# version 25
# baseline (speedup 1.0000x reference)
import numpy as np
import concourse.bass as bass
import concourse.mybir as mybir
from concourse.bass_utils import run_bass_kernel_spmd

F32 = mybir.dt.float32
AF = mybir.ActivationFunctionType
ALU = mybir.AluOpType

DM = 1024
DI_H = 1024
NST = 16
TC = 256
NTT = TC // 128
NDT = DI_H // 128
NMT = DM // 128
E = 16
HDD = 128
IFF = 512
NH = 4

C_ONE, C_NEG1, C_EPS, C_BIG, C_ZERO = 0, 1, 2, 3, 4


class Sem:
    def __init__(self, h):
        self.h = h
        self.n = 0

    def inc(self, ins, k):
        if ins is not None:
            ins.then_inc(self.h, k)
        self.n += k


class Eng:
    def __init__(self, eng, active, track=True, self_sem=None):
        self.e = eng if active else None
        self.dirty = set()
        self.reads = set()
        self.track = track
        self.self_sem = self_sem

    def wait(self, sem):
        if self.e is not None and sem.n > 0:
            self.e.wait_ge(sem.h, sem.n)

    def wait_v(self, sem, v):
        if self.e is not None and v > 0:
            self.e.wait_ge(sem.h, v)

    def free(self, sem):
        i = None
        if self.e is not None:
            self.e.drain()
            i = self.e.nop()
        sem.inc(i, 1)

    def op(self, method, outs, ins, *args, **kw):
        if self.e is None:
            return None
        def nm(lst):
            out = set()
            for a in lst:
                try:
                    out.add(a.name)
                except Exception:
                    pass
            return out
        if self.track:
            no, ni = nm(outs), nm(ins)
            if (no & (self.dirty | self.reads)) or (ni & self.dirty):
                if self.self_sem is not None:
                    self.e.wait_ge(self.self_sem.h, self.self_sem.n)
                else:
                    self.e.drain()
                self.dirty.clear()
                self.reads.clear()
            ins_obj = getattr(self.e, method)(*args, **kw)
            self.dirty |= no
            self.reads |= ni
            return ins_obj
        return getattr(self.e, method)(*args, **kw)


def build_nc(T=4096, n_cores=8):
    NCH = T // TC
    groups = [[2 * i, 2 * i + 1] for i in range(n_cores // 2)]
    nc = bass.Bass("TRN2", target_bir_lowering=False, debug=False)

    X = nc.dram_tensor("X", [T, DM], F32, kind="ExternalInput")
    WIN = nc.dram_tensor("WIN", [16, 128, 8, 128], F32, kind="ExternalInput")
    LNG = nc.dram_tensor("LNG", [128, 8], F32, kind="ExternalInput")
    LNB = nc.dram_tensor("LNB", [128, 8], F32, kind="ExternalInput")
    CONVW = nc.dram_tensor("CONVW", [128, 8, 4], F32, kind="ExternalInput")
    CONVB = nc.dram_tensor("CONVB", [128, 8], F32, kind="ExternalInput")
    WXP = nc.dram_tensor("WXP", [128, 8, 33], F32, kind="ExternalInput")
    WDT = nc.dram_tensor("WDT", [128, 8], F32, kind="ExternalInput")
    BDT = nc.dram_tensor("BDT", [128, 8], F32, kind="ExternalInput")
    ALOG = nc.dram_tensor("ALOG", [128, 8, NST], F32, kind="ExternalInput")
    DDIN = nc.dram_tensor("DDIN", [128, 8], F32, kind="ExternalInput")
    WOUT = nc.dram_tensor("WOUT", [128, 8, DM], F32, kind="ExternalInput")
    WM = nc.dram_tensor("WM", [128, 8, NH * 128], F32, kind="ExternalInput")
    BM = nc.dram_tensor("BM", [128, NH], F32, kind="ExternalInput")
    EMB = nc.dram_tensor("EMB", [128, NST], F32, kind="ExternalInput")
    W1 = nc.dram_tensor("W1", [128, E, IFF], F32, kind="ExternalInput")
    B1 = nc.dram_tensor("B1", [128, E, 4], F32, kind="ExternalInput")
    W2 = nc.dram_tensor("W2", [128, E, 4, 128], F32, kind="ExternalInput")
    B2 = nc.dram_tensor("B2", [128, E], F32, kind="ExternalInput")
    WO = nc.dram_tensor("WO", [128, NH, DM], F32, kind="ExternalInput")
    BOLH = nc.dram_tensor("BOLH", [1, 8 * 128], F32, kind="ExternalInput")
    ONES1 = nc.dram_tensor("ONES1", [1, 128], F32, kind="ExternalInput")
    ONESR = nc.dram_tensor("ONESR", [1, TC], F32, kind="ExternalInput")
    IDENT = nc.dram_tensor("IDENT", [128, 128], F32, kind="ExternalInput")
    CONSTS = nc.dram_tensor("CONSTS", [128, 8], F32, kind="ExternalInput")
    OUT = nc.dram_tensor("OUT", [T, DM], F32, kind="ExternalOutput")
    ccxi = nc.dram_tensor("ccxi", [33, TC], F32)
    ccxo = nc.dram_tensor("ccxo", [33, TC], F32)
    ccmi = nc.dram_tensor("ccmi", [128, 8 * TC], F32)
    ccmo = nc.dram_tensor("ccmo", [128, 8 * TC], F32)
    ccfi = nc.dram_tensor("ccfi", [128, 8 * TC], F32)
    ccfo = nc.dram_tensor("ccfo", [128, 8 * TC], F32)

    from contextlib import ExitStack
    es = ExitStack()
    SB = lambda n_, s: es.enter_context(nc.sbuf_tensor(n_, s, F32))
    PS = lambda n_: es.enter_context(nc.psum_tensor(n_, [128, 512], F32))
    SEM = lambda n_: Sem(es.enter_context(nc.semaphore(n_)))

    with es:
        b_wos = SB("b_wos", [128, 2, DM])
        b_wms = SB("b_wms", [128, 3, NH * 128])
        b_wo = SB("b_wo", [128, NH, DM])
        b_wxp = SB("b_wxp", [128, 8, 33])
        b_A = SB("b_A", [128, 8, NST])
        b_convw = SB("b_convw", [128, 8, 4])
        b_convb = SB("b_convb", [128, 8])
        b_wdt = SB("b_wdt", [128, 8])
        b_bdt = SB("b_bdt", [128, 8])
        b_dd = SB("b_dd", [128, 8])
        b_lng = SB("b_lng", [128, 8])
        b_lnb = SB("b_lnb", [128, 8])
        b_bm = SB("b_bm", [128, NH])
        b_emb = SB("b_emb", [128, NST])
        b_b1 = SB("b_b1", [128, E, 4])
        b_b2 = SB("b_b2", [128, E])
        b_bolh = SB("b_bolh", [1, 8 * 128])
        b_ones1 = SB("b_ones1", [1, 128])
        b_onesr = SB("b_onesr", [1, TC])
        b_ident = SB("b_ident", [128, 128])
        b_c = SB("b_c", [128, 8])
        b_bvec = SB("b_bvec", [128, 16])
        b_win = SB("b_win", [128, 2, 8, 128])
        b_w1s = SB("b_w1s", [128, 3, IFF])
        b_w2s = SB("b_w2s", [128, 3, 4, 128])
        b_x0 = SB("b_x0", [128, NTT, DM])
        b_xn = SB("b_xn", [128, NTT, DM])
        b_tma = SB("b_tma", [128, NTT, DM])
        b_x1 = SB("b_x1", [128, NTT, DM])
        b_s1T = SB("b_s1T", [128, 8, TC])
        b_zs = SB("b_zs", [128, 8, TC])
        b_xc = SB("b_xc", [128, 8, TC])
        b_dt = SB("b_dt", [128, 8, TC])
        b_yA = SB("b_yA", [128, 8, TC])
        b_yB = SB("b_yB", [128, 8, TC])
        b_po = SB("b_po", [128, 8, TC])
        b_hf = SB("b_hf", [128, NH, TC])
        b_hm = SB("b_hm", [128, 2, 4, TC])
        b_cmA = SB("b_cmA", [128, NH, TC])
        b_cmB = SB("b_cmB", [128, NH, TC])
        b_eob = SB("b_eob", [128, 4, TC])
        b_tmp = SB("b_tmp", [128, TC])
        b_gt = SB("b_gt", [128, TC])
        b_mbc = SB("b_mbc", [128, 4, TC])
        b_da = SB("b_da", [128, 8, TC])
        b_dbx = SB("b_dbx", [128, 2, 4, TC])
        b_xhe = SB("b_xhe", [128, TC + 8])
        b_cv0 = SB("b_cv0", [128, TC])
        b_cv1 = SB("b_cv1", [128, TC])
        b_ubc = SB("b_ubc", [128, TC])
        b_tail = SB("b_tail", [128, 8, 3])
        b_state = SB("b_state", [128, 128])
        b_xpj = SB("b_xpj", [33, TC])
        b_lg = SB("b_lg", [16, TC])
        b_mem = SB("b_mem", [16, NH * TC])
        b_lt = SB("b_lt", [128, NST])
        b_w16 = SB("b_w16", [128, NST])
        b_i2 = SB("b_i2", [128, NST])
        b_t1 = SB("b_t1", [128, NST])
        b_ltm = SB("b_ltm", [128, NST])
        b_st = SB("b_st", [128, 12])
        b_mv = SB("b_mv", [128, 8])
        b_rs = SB("b_rs", [128, 8])
        b_m1 = SB("b_m1", [128, 4])
        b_rec = SB("b_rec", [128, 1])
        pk = [PS(f"pk{i}") for i in range(8)]

        s_ld = SEM("s_ld")
        st_x = (SEM("s_x0a"), SEM("s_x0b"))
        st_win = (SEM("s_wina"), SEM("s_winb"))
        st_wo = (SEM("s_woa"), SEM("s_wob"))
        st_wm = (SEM("s_wma"), SEM("s_wmb"))
        st_w1 = (SEM("s_w1a"), SEM("s_w1b"))
        st_w2 = (SEM("s_w2a"), SEM("s_w2b"))
        s_wf = SEM("s_wf")
        s_wof = SEM("s_wof")
        s_wmf = SEM("s_wmf")
        s_w1f = SEM("s_w1f")
        s_w2f = SEM("s_w2f")
        s_cci = SEM("s_cci")
        s_cc = SEM("s_cc")
        s_cco = SEM("s_cco")
        s_pe = SEM("s_pe")
        s_ve = SEM("s_ve")
        s_se = SEM("s_se")
        s_out = SEM("s_out")
        s_gp = SEM("s_gp")
        s_xf = SEM("s_xf")

        block = es.enter_context(nc.Block())

        def gen(active, sync_e, tensor_e, vector_e, scalar_e, gpsimd_e):
            SY = Eng(sync_e, active == "sync")
            PE = Eng(tensor_e, active == "tensor", track=False)
            VE = Eng(vector_e, active == "vector", self_sem=s_ve)
            SE = Eng(scalar_e, active == "scalar", self_sem=s_se)
            GP = Eng(gpsimd_e, active == "gpsimd")

            ones = b_c[:, C_ONE:C_ONE + 1]
            neg1 = b_c[:, C_NEG1:C_NEG1 + 1]
            epsc = b_c[:, C_EPS:C_EPS + 1]
            bigc = b_c[:, C_BIG:C_BIG + 1]

            def sload(pair, k, dst, src):
                dma(dst, src, pair[k % 2])

            def swait(E, pair, k):
                E.wait_v(pair[k % 2], 16 * (k // 2 + 1))

            def dma(dst, src, sem):
                SY.wait(sem)
                i = None
                if SY.e is not None:
                    i = SY.e.dma_start(dst, src)
                sem.inc(i, 16)

            dma(b_wo[:, :, :], WO[:, :, :], s_ld)
            dma(b_wxp[:, :, :], WXP[:, :, :], s_ld)
            dma(b_A[:, :, :], ALOG[:, :, :], s_ld)
            dma(b_convw[:, :, :], CONVW[:, :, :], s_ld)
            dma(b_convb[:, :], CONVB[:, :], s_ld)
            dma(b_wdt[:, :], WDT[:, :], s_ld)
            dma(b_bdt[:, :], BDT[:, :], s_ld)
            dma(b_dd[:, :], DDIN[:, :], s_ld)
            dma(b_lng[:, :], LNG[:, :], s_ld)
            dma(b_lnb[:, :], LNB[:, :], s_ld)
            dma(b_bm[:, :], BM[:, :], s_ld)
            dma(b_emb[:, :], EMB[:, :], s_ld)
            dma(b_b1[:, :, :], B1[:, :, :], s_ld)
            dma(b_b2[:, :], B2[:, :], s_ld)
            dma(b_bolh[:, :], BOLH[:, :], s_ld)
            dma(b_ones1[:, :], ONES1[:, :], s_ld)
            dma(b_onesr[:, :], ONESR[:, :], s_ld)
            dma(b_ident[:, :], IDENT[:, :], s_ld)
            dma(b_c[:, :], CONSTS[:, :], s_ld)
            n_pre = s_ld.n

            SE.wait(s_ld)
            VE.wait(s_ld)
            for d in range(8):
                SE.wait(s_ve)
                i = SE.op("activation", [b_cv0[:, 0:NST]], [b_A[:, d, :]],
                          b_cv0[:, 0:NST], b_A[:, d, :], AF.Exp)
                s_se.inc(i, 1)
                VE.wait(s_se)
                i = VE.op("tensor_copy", [b_A[:, d, :]], [b_cv0[:, 0:NST]],
                          b_A[:, d, :], b_cv0[:, 0:NST])
                s_ve.inc(i, 1)
            i = VE.op("memset", [b_tail[:, :, :]], [], b_tail[:, :, :], 0.0)
            s_ve.inc(i, 1)
            i = VE.op("memset", [b_state[:, :]], [], b_state[:, :], 0.0)
            s_ve.inc(i, 1)

            PE.wait(s_ld)
            GP.wait(s_ld)
            sload(st_win, 0, b_win[:, 0, :, :], WIN[0, :, :, :])
            sload(st_win, 1, b_win[:, 1, :, :], WIN[1, :, :, :])

            out_dmas = 0
            tma_mark = [0]
            eob_mark = [0, 0, 0, 0]
            mbc_mark = [0, 0, 0, 0]
            bmark = {}

            def bank_wait(i):
                m = bmark.get(i)
                if m is not None:
                    PE.wait_v(m[0], m[1])

            def bank_release(i, sem):
                bmark[i] = (sem, sem.n)

            ln_rot = [0]

            def layernorm_tm(src_tile, dst_tile, pe_mark=None):
                sl = (ln_rot[0] % 4) * 2
                ln_rot[0] += 1
                mv = b_mv[:, sl:sl + 2]
                i = VE.op("bn_stats", [b_st[:, 0:6]], [src_tile[:, 0:512]],
                          b_st[:, 0:6], src_tile[:, 0:512])
                s_ve.inc(i, 1)
                i = VE.op("bn_stats", [b_st[:, 6:12]], [src_tile[:, 512:1024]],
                          b_st[:, 6:12], src_tile[:, 512:1024])
                s_ve.inc(i, 1)
                i = VE.op("bn_aggr", [b_mv[:, 0:2]], [b_st[:, :]],
                          mv, b_st[:, :])
                s_ve.inc(i, 1)
                SE.wait(s_ve)
                i = SE.op("activation", [b_rs[:, 1:2]], [b_mv[:, 0:2], epsc],
                          b_rs[:, sl + 1:sl + 2], mv[:, 1:2], AF.Sqrt,
                          bias=epsc)
                s_se.inc(i, 1)
                VE.wait(s_se)
                i = VE.op("reciprocal", [b_rs[:, 0:1]], [b_rs[:, 1:2]],
                          b_rs[:, sl:sl + 1], b_rs[:, sl + 1:sl + 2])
                s_ve.inc(i, 1)
                if pe_mark is not None:
                    VE.wait_v(s_pe, pe_mark)
                i = VE.op("tensor_scalar", [dst_tile],
                          [src_tile, b_mv[:, 0:2], b_rs[:, 0:1]],
                          dst_tile, src_tile, mv[:, 0:1], b_rs[:, sl:sl + 1],
                          ALU.subtract, ALU.mult)
                s_ve.inc(i, 1)

            hm4_mark = {}
            hmq = [0]
            scan_ue_mark = [0]

            def seg_loads_ln(ch):
                if ch > 0:
                    SY.wait(s_xf)
                    SE.wait(s_out)
                    VE.wait(s_out)
                t0 = ch * TC
                for tt in range(NTT):
                    sload(st_x, ch * NTT + tt, b_x0[:, tt, :],
                          X[t0 + tt * 128: t0 + (tt + 1) * 128, :])
                VE.wait(s_ld)
                SE.wait(s_ld)
                for tt in range(NTT):
                    swait(VE, st_x, ch * NTT + tt)
                    swait(SE, st_x, ch * NTT + tt)
                    layernorm_tm(b_x0[:, tt, :], b_xn[:, tt, :])
                    layernorm_tm(b_xn[:, tt, :], b_tma[:, tt, :],
                                 pe_mark=tma_mark[0])
                    for d in range(NMT):
                        bki = 6 + (d % 2)
                        bk = pk[bki]
                        if d == 0:
                            PE.wait(s_ve)
                        bank_wait(bki)
                        i = PE.op("transpose", [bk[:, 0:128]],
                                  [b_tma[:, tt, :], b_ident[:, :]],
                                  bk[:, 0:128],
                                  b_tma[:, tt, d * 128:(d + 1) * 128],
                                  b_ident[:, :])
                        s_pe.inc(i, 1)
                        SE.wait(s_pe)
                        SE.wait_v(s_ve, scan_ue_mark[0])
                        i = SE.op("activation",
                                  [b_s1T[:, d, tt * 128:(tt + 1) * 128]],
                                  [bk[:, 0:128], b_lng[:, d:d + 1]],
                                  b_s1T[:, d, tt * 128:(tt + 1) * 128],
                                  bk[:, 0:128], AF.Identity,
                                  scale=b_lng[:, d:d + 1])
                        s_se.inc(i, 1)
                        bank_release(bki, s_se)
                    tma_mark[0] = s_pe.n
                mark_s1t[0] = s_se.n

            def seg_win_mm(ch, j):
                g = ch * 16 + j
                slot = g % 2
                bki = 4 + (j % 2)
                bk = pk[bki]
                swait(PE, st_win, g)
                bank_wait(bki)
                if j == 0:
                    PE.wait_v(s_se, mark_s1t[0])
                for kc in range(8):
                    i = PE.op("matmul", [bk[:, 0:TC]],
                              [b_win[:, slot, kc, :], b_s1T[:, kc, :]],
                              bk[:, 0:TC], b_win[:, slot, kc, :],
                              b_s1T[:, kc, :],
                              start=(kc == 0), stop=(kc == 7))
                    if kc == 3 and g >= 1:
                        s_wf.inc(i, 1)
                    else:
                        s_pe.inc(i, 1)
                if ch == 0 and PRE0:
                    bank_wait(7)
                    for kc in range(8):
                        i = PE.op("matmul", [pk[7][:, 0:1]],
                                  [b_win[:, slot, kc, :], b_lnb[:, kc:kc + 1]],
                                  pk[7][:, 0:1], b_win[:, slot, kc, :],
                                  b_lnb[:, kc:kc + 1],
                                  start=(kc == 0), stop=(kc == 7))
                        s_pe.inc(i, 1)
                    SE.wait(s_pe)
                    i = SE.op("copy", [b_bvec[:, j:j + 1]], [pk[7][:, 0:1]],
                              b_bvec[:, j:j + 1], pk[7][:, 0:1])
                    s_se.inc(i, 1)
                    bank_release(7, s_se)
                nj = g + 1
                if g > 0 and nj < NCH * 16:
                    SY.wait_v(s_wf, g)
                    sload(st_win, nj, b_win[:, nj % 2, :, :], WIN[nj % 16, :, :, :])

            def seg_win_post(ch, j):
                g = ch * 16 + j
                bki = 4 + (j % 2)
                bk = pk[bki]
                SE.wait(s_pe)
                if j < 8:
                    d = j
                    i = SE.op("activation", [b_xhe[:, 3:3 + TC]],
                              [bk[:, 0:TC], b_bvec[:, j:j + 1]],
                              b_xhe[:, 3:3 + TC], bk[:, 0:TC], AF.Identity,
                              bias=b_bvec[:, j:j + 1])
                    s_se.inc(i, 1)
                    bank_release(bki, s_se)
                    VE.wait(s_se)
                    i = VE.op("tensor_copy", [b_xhe[:, 0:3]], [b_tail[:, d, :]],
                              b_xhe[:, 0:3], b_tail[:, d, :])
                    s_ve.inc(i, 1)
                    i = VE.op("tensor_copy", [b_tail[:, d, :]],
                              [b_xhe[:, TC:TC + 3]],
                              b_tail[:, d, :], b_xhe[:, TC:TC + 3])
                    s_ve.inc(i, 1)
                    i = VE.op("tensor_scalar", [b_cv0[:, :]],
                              [b_xhe[:, 0:TC], b_convw[:, d, 0:1], b_convb[:, d:d + 1]],
                              b_cv0[:, :], b_xhe[:, 0:TC], b_convw[:, d, 0:1],
                              b_convb[:, d:d + 1], ALU.mult, ALU.add)
                    s_ve.inc(i, 1)
                    i = VE.op("scalar_tensor_tensor", [b_cv1[:, :]],
                              [b_xhe[:, 1:1 + TC], b_convw[:, d, 1:2], b_cv0[:, :]],
                              b_cv1[:, :], b_xhe[:, 1:1 + TC],
                              b_convw[:, d, 1:2], b_cv0[:, :], ALU.mult, ALU.add)
                    s_ve.inc(i, 1)
                    i = VE.op("scalar_tensor_tensor", [b_tmp[:, :]],
                              [b_xhe[:, 2:2 + TC], b_convw[:, d, 2:3], b_cv1[:, :]],
                              b_tmp[:, :], b_xhe[:, 2:2 + TC],
                              b_convw[:, d, 2:3], b_cv1[:, :], ALU.mult, ALU.add)
                    s_ve.inc(i, 1)
                    i = VE.op("scalar_tensor_tensor", [b_cv0[:, :]],
                              [b_xhe[:, 3:3 + TC], b_convw[:, d, 3:4], b_tmp[:, :]],
                              b_cv0[:, :], b_xhe[:, 3:3 + TC],
                              b_convw[:, d, 3:4], b_tmp[:, :], ALU.mult, ALU.add)
                    s_ve.inc(i, 1)
                    SE.wait(s_ve)
                    if d == 0:
                        SE.wait_v(s_pe, mark_outtr[0])
                    i = SE.op("activation", [b_xc[:, d, :]], [b_cv0[:, :]],
                              b_xc[:, d, :], b_cv0[:, :], AF.Silu)
                    s_se.inc(i, 1)
                    VE.wait(s_se)
                    VE.wait(s_gp)
                    i = VE.op("tensor_scalar_mul", [b_yA[:, d, :]],
                              [b_xc[:, d, :], b_dd[:, d:d + 1]],
                              b_yA[:, d, :], b_xc[:, d, :], b_dd[:, d:d + 1])
                    s_ve.inc(i, 1)
                    PE.wait(s_se)
                    if d == 0:
                        bank_wait(6)
                    i = PE.op("matmul", [pk[6][0:33, 0:TC]],
                              [b_wxp[:, d, :], b_xc[:, d, :]],
                              pk[6][0:33, 0:TC], b_wxp[:, d, :], b_xc[:, d, :],
                              start=(d == 0), stop=(d == 7))
                    s_pe.inc(i, 1)
                    if d == 7:
                        VE.wait(s_pe)
                        i = VE.op("tensor_copy", [b_xpj[:, :]],
                                  [pk[6][0:33, 0:TC]],
                                  b_xpj[:, :], pk[6][0:33, 0:TC])
                        s_ve.inc(i, 1)
                        bank_release(6, s_ve)
                        SY.wait(s_ve)
                        dma(ccxi[:, :], b_xpj[:, :], s_cci)
                        GP.wait(s_cci)
                        i = None
                        if GP.e is not None:
                            i = GP.e.collective_compute(
                                "AllReduce", ALU.add, replica_groups=groups,
                                ins=[ccxi[:, :]], outs=[ccxo[:, :]])
                        s_cc.inc(i, 1)
                        SY.wait(s_cc)
                        dma(b_xpj[:, :], ccxo[:, :], s_cco)
                else:
                    d = j - 8
                    if d == 0:
                        SE.wait(s_ve)
                    i = SE.op("activation", [b_zs[:, d, :]],
                              [bk[:, 0:TC], b_bvec[:, j:j + 1]],
                              b_zs[:, d, :], bk[:, 0:TC], AF.Silu,
                              bias=b_bvec[:, j:j + 1])
                    s_se.inc(i, 1)
                    bank_release(bki, s_se)
                if j == 15:
                    mark_win[0] = s_pe.n

            def seg_dt(ch, d):
                if d == 0:
                    SE.wait(s_out)
                    PE.wait(s_cco)
                    bank_wait(6)
                    i = PE.op("matmul", [pk[6][:, 0:TC]],
                              [b_ident[:, :], b_xpj[:, :]],
                              pk[6][:, 0:TC],
                              b_ident[0:33, 32:33].broadcast_to([33, 128]),
                              b_xpj[:, :], start=True, stop=True)
                    s_pe.inc(i, 1)
                    SE.wait(s_pe)
                    i = SE.op("copy", [b_ubc[:, :]], [pk[6][:, 0:TC]],
                              b_ubc[:, :], pk[6][:, 0:TC])
                    s_se.inc(i, 1)
                    bank_release(6, s_se)
                SE.wait(s_ve)
                i = SE.op("activation", [b_cv0[:, :]],
                          [b_ubc[:, :], b_wdt[:, d:d + 1], b_bdt[:, d:d + 1]],
                          b_cv0[:, :], b_ubc[:, :], AF.Sigmoid,
                          scale=b_wdt[:, d:d + 1], bias=b_bdt[:, d:d + 1])
                s_se.inc(i, 1)
                i = SE.op("activation", [b_dt[:, d, :]], [b_cv0[:, :]],
                          b_dt[:, d, :], b_cv0[:, :], AF.Ln)
                s_se.inc(i, 1)
                VE.wait(s_se)
                if d == 0:
                    VE.wait_v(s_pe, mark_win[0])
                i = VE.op("tensor_mul", [b_s1T[:, d, :]], [b_dt[:, d, :], b_xc[:, d, :]],
                          b_s1T[:, d, :], b_dt[:, d, :], b_xc[:, d, :])
                s_ve.inc(i, 1)

            def seg_scan_n(ch, n):
                bkn = 4 + (n % 2)
                bank_wait(bkn)
                if n == 0:
                    PE.wait(s_se)
                    PE.wait(s_ve)
                    scan_state["marks"] = []
                i = PE.op("matmul", [pk[bkn][:, 0:TC]],
                          [b_ident[:, :], b_xpj[:, :]],
                          pk[bkn][:, 0:TC],
                          b_ident[0:33, n:n + 1].broadcast_to([33, 128]),
                          b_xpj[:, :], start=True, stop=True)
                s_pe.inc(i, 1)
                i = PE.op("matmul", [pk[bkn][:, TC:2 * TC]],
                          [b_ident[:, :], b_xpj[:, :]],
                          pk[bkn][:, TC:2 * TC],
                          b_ident[0:33, NST + n:NST + n + 1].broadcast_to([33, 128]),
                          b_xpj[:, :], start=True, stop=True)
                s_pe.inc(i, 1)
                marks = scan_state["marks"]
                ysrc, ydst = (b_yA, b_yB) if n % 2 == 0 else (b_yB, b_yA)
                dab = b_da if n % 2 == 0 else b_po
                for d in range(8):
                    k = n * 8 + d
                    if k >= 16:
                        SE.wait_v(s_ve, marks[k - 16])
                    i = SE.op("activation", [dab[:, d, :]],
                              [b_dt[:, d, :], b_A[:, d, n:n + 1]],
                              dab[:, d, :], b_dt[:, d, :], AF.Exp,
                              scale=b_A[:, d, n:n + 1])
                    s_se.inc(i, 1)
                VE.wait(s_pe)
                VE.wait(s_se)
                for dh in range(2):
                    d0 = dh * 4
                    dbx = b_dbx[:, dh, :, :]
                    VE.wait(s_gp)
                    i = VE.op("tensor_tensor", [dbx],
                              [b_s1T[:, 0, :]],
                              dbx, b_s1T[:, d0:d0 + 4, :],
                              pk[bkn][:, 0:TC].unsqueeze(1).broadcast_to([128, 4, TC]),
                              ALU.mult)
                    s_ve.inc(i, 1)
                    for dd in range(4):
                        d = d0 + dd
                        strip = n * 8 + d
                        i = VE.op("tensor_tensor_scan", [b_xc[:, d, :]],
                                  [dab[:, d, :], dbx[:, dd, :],
                                   b_state[:, strip:strip + 1]],
                                  b_xc[:, d, :], dab[:, d, :], dbx[:, dd, :],
                                  b_state[:, strip:strip + 1], ALU.mult, ALU.add)
                        s_ve.inc(i, 1)
                        marks.append(s_ve.n)
                    i = VE.op("tensor_tensor", [dbx], [b_xc[:, 0, :]],
                              dbx, b_xc[:, d0:d0 + 4, :],
                              pk[bkn][:, TC:2 * TC].unsqueeze(1).broadcast_to([128, 4, TC]),
                              ALU.mult)
                    s_ve.inc(i, 1)
                    GP.wait(s_ve)
                    i = GP.op("tensor_tensor", [ydst[:, d0, :]],
                              [ysrc[:, 0, :], dbx[:, 0, :]],
                              ydst[:, d0:d0 + 4, :], ysrc[:, d0:d0 + 4, :],
                              dbx, ALU.add)
                    s_gp.inc(i, 1)
                bank_release(bkn, s_ve)
                i = VE.op("tensor_copy", [b_state[:, n * 8:(n + 1) * 8]],
                          [b_xc[:, :, TC - 1:TC]],
                          b_state[:, n * 8:(n + 1) * 8], b_xc[:, :, TC - 1:TC])
                s_ve.inc(i, 1)
                if n == NST - 1:
                    scan_ue_mark[0] = s_ve.n

            def seg_wout(ch):
                VE.wait(s_gp)
                for kc in range(8):
                    i = VE.op("tensor_mul", [b_po[:, kc, :]],
                              [b_yA[:, kc, :], b_zs[:, kc, :]],
                              b_po[:, kc, :], b_yA[:, kc, :], b_zs[:, kc, :])
                    s_ve.inc(i, 1)
                g2 = ch * 8
                SY.wait_v(s_wof, max(0, g2 - 1))
                sload(st_wo, g2, b_wos[:, g2 % 2, :], WOUT[:, 0, :])
                for kc in range(8):
                    g2 = ch * 8 + kc
                    swait(PE, st_wo, g2)
                    if kc == 0:
                        PE.wait(s_ve)
                        for bi in range(8):
                            bank_wait(bi)
                    for m in range(8):
                        i = PE.op("matmul", [pk[m][:, 0:TC]],
                                  [b_wos[:, g2 % 2, m * 128:(m + 1) * 128],
                                   b_po[:, kc, :]],
                                  pk[m][:, 0:TC],
                                  b_wos[:, g2 % 2, m * 128:(m + 1) * 128],
                                  b_po[:, kc, :], start=(kc == 0), stop=(kc == 7))
                        if m == 3 and g2 >= 1:
                            s_wof.inc(i, 1)
                        else:
                            s_pe.inc(i, 1)
                    if kc + 1 < 8:
                        ng = g2 + 1
                        SY.wait_v(s_wof, g2)
                        sload(st_wo, ng, b_wos[:, ng % 2, :], WOUT[:, kc + 1, :])
                SE.wait(s_pe)
                for m in range(8):
                    i = SE.op("copy", [b_po[:, m, :]], [pk[m][:, 0:TC]],
                              b_po[:, m, :], pk[m][:, 0:TC])
                    s_se.inc(i, 1)
                    bank_release(m, s_se)
                for tt in range(NTT):
                    for d in range(NMT):
                        bki = 6 + (d % 2)
                        bk = pk[bki]
                        if tt == 0 and d == 0:
                            PE.wait(s_se)
                        bank_wait(bki)
                        i = PE.op("transpose", [bk[:, 0:128]],
                                  [b_po[:, d, :], b_ident[:, :]],
                                  bk[:, 0:128],
                                  b_po[:, d, tt * 128:(tt + 1) * 128],
                                  b_ident[:, :])
                        s_pe.inc(i, 1)
                        SE.wait(s_pe)
                        i = SE.op("copy", [b_yA[:, 0, :]], [bk[:, 0:128]],
                                  bass.AP(b_yA, tt * DM + d * 128,
                                          [[8 * TC, 128], [1, 128]]),
                                  bk[:, 0:128])
                        s_se.inc(i, 1)
                        bank_release(bki, s_se)
                SY.wait(s_se)
                dma(ccmi[:, :], b_yA[:, :, :], s_cci)
                GP.wait(s_cci)
                i = None
                if GP.e is not None:
                    i = GP.e.collective_compute(
                        "AllReduce", ALU.add, replica_groups=groups,
                        ins=[ccmi[:, :]], outs=[ccmo[:, :]])
                s_cc.inc(i, 1)
                SY.wait(s_cc)
                dma(b_yA[:, :, :], ccmo[:, :], s_cco)

            def seg_sandwich(ch):
                for tt in range(NTT):
                    VE.wait(s_cco)
                    i = VE.op("tensor_tensor", [b_tma[:, tt, :]],
                              [b_xn[:, tt, :], b_yA[:, 0, :]],
                              b_tma[:, tt, :], b_xn[:, tt, :],
                              bass.AP(b_yA, tt * DM, [[8 * TC, 128], [1, DM]]),
                              ALU.subtract)
                    s_ve.inc(i, 1)
                    layernorm_tm(b_tma[:, tt, :], b_xn[:, tt, :])
                    i = VE.op("tensor_add", [b_x1[:, tt, :]],
                              [b_xn[:, tt, :], b_x0[:, tt, :]],
                              b_x1[:, tt, :], b_xn[:, tt, :], b_x0[:, tt, :])
                    s_ve.inc(i, 1)
                    layernorm_tm(b_x1[:, tt, :], b_x0[:, tt, :])
                    for d in range(NMT):
                        bki = 6 + (d % 2)
                        bk = pk[bki]
                        PE.wait(s_ve)
                        bank_wait(bki)
                        i = PE.op("transpose", [bk[:, 0:128]],
                                  [b_x0[:, tt, :], b_ident[:, :]],
                                  bk[:, 0:128],
                                  b_x0[:, tt, d * 128:(d + 1) * 128],
                                  b_ident[:, :])
                        s_pe.inc(i, 1)
                        SE.wait(s_pe)
                        i = SE.op("copy",
                                  [b_s1T[:, d, tt * 128:(tt + 1) * 128]],
                                  [bk[:, 0:128]],
                                  b_s1T[:, d, tt * 128:(tt + 1) * 128],
                                  bk[:, 0:128])
                        s_se.inc(i, 1)
                        bank_release(bki, s_se)
                mark_fin[0] = s_se.n
                PE.free(s_xf)

            def seg_wm_router(ch):
                g2 = ch * 8
                SY.wait_v(s_wmf, max(0, g2 - 1))
                sload(st_wm, g2, b_wms[:, g2 % 2, :], WM[:, 0, :])
                for kc in range(8):
                    g2 = ch * 8 + kc
                    swait(PE, st_wm, g2)
                    if kc == 0:
                        PE.wait_v(s_se, mark_fin[0])
                        for bi in range(NH):
                            bank_wait(bi)
                    for j in range(NH):
                        i = PE.op("matmul", [pk[j][:, 0:TC]],
                                  [b_wms[:, g2 % 2, j * 128:(j + 1) * 128],
                                   b_s1T[:, kc, :]],
                                  pk[j][:, 0:TC],
                                  b_wms[:, g2 % 2, j * 128:(j + 1) * 128],
                                  b_s1T[:, kc, :], start=(kc == 0), stop=(kc == 7))
                        if j == 2 and g2 >= 1:
                            s_wmf.inc(i, 1)
                        else:
                            s_pe.inc(i, 1)
                    if kc + 1 < 8:
                        ng = g2 + 1
                        SY.wait_v(s_wmf, g2)
                        sload(st_wm, ng, b_wms[:, ng % 2, :], WM[:, kc + 1, :])
                SE.wait(s_pe)
                for j in range(NH):
                    i = SE.op("activation", [b_hf[:, j, :]],
                              [pk[j][:, 0:TC], b_bm[:, j:j + 1]],
                              b_hf[:, j, :], pk[j][:, 0:TC], AF.Identity,
                              bias=b_bm[:, j:j + 1])
                    s_se.inc(i, 1)
                    bank_release(j, s_se)
                for j in range(NH):
                    PE.wait(s_se)
                    bank_wait(2)
                    i = PE.op("matmul", [pk[2][0:NST, 0:TC]],
                              [b_emb[:, :], b_hf[:, j, :]],
                              pk[2][0:NST, 0:TC], b_emb[:, :], b_hf[:, j, :],
                              start=True, stop=True)
                    s_pe.inc(i, 1)
                    SE.wait(s_pe)
                    i = SE.op("copy", [b_lg[:, :]], [pk[2][0:NST, 0:TC]],
                              b_lg[:, :], pk[2][0:NST, 0:TC])
                    s_se.inc(i, 1)
                    bank_release(2, s_se)
                    for tt2 in range(NTT):
                        PE.wait(s_se)
                        bank_wait(3)
                        i = PE.op("transpose", [pk[3][0:128, 0:NST]],
                                  [b_lg[:, :], b_ident[:, :]],
                                  pk[3][0:128, 0:NST],
                                  b_lg[:, tt2 * 128:(tt2 + 1) * 128],
                                  b_ident[0:NST, 0:NST])
                        s_pe.inc(i, 1)
                        VE.wait(s_pe)
                        i = VE.op("tensor_copy", [b_lt[:, :]], [pk[3][0:128, 0:NST]],
                                  b_lt[:, :], pk[3][0:128, 0:NST])
                        s_ve.inc(i, 1)
                        bank_release(3, s_ve)
                        i = VE.op("reduce_max", [b_m1[:, 0:1]], [b_lt[:, :]],
                                  b_m1[:, 0:1], b_lt[:, :], mybir.AxisListType.X)
                        s_ve.inc(i, 1)
                        i = VE.op("tensor_scalar_mul", [b_m1[:, 1:2]],
                                  [b_m1[:, 0:1], neg1],
                                  b_m1[:, 1:2], b_m1[:, 0:1], neg1)
                        s_ve.inc(i, 1)
                        SE.wait(s_ve)
                        i = SE.op("activation", [b_w16[:, :], b_m1[:, 3:4]],
                                  [b_lt[:, :], b_m1[:, 1:2]],
                                  b_w16[:, :], b_lt[:, :], AF.Exp,
                                  bias=b_m1[:, 1:2], accum_out=b_m1[:, 3:4])
                        s_se.inc(i, 1)
                        VE.wait(s_se)
                        i = VE.op("reciprocal", [b_rec[:, :]], [b_m1[:, 3:4]],
                                  b_rec[:, :], b_m1[:, 3:4])
                        s_ve.inc(i, 1)
                        i = VE.op("tensor_single_scalar", [b_i2[:, :]],
                                  [b_lt[:, :], b_m1[:, 0:1]],
                                  b_i2[:, :], b_lt[:, :], b_m1[:, 0:1], ALU.is_ge)
                        s_ve.inc(i, 1)
                        i = VE.op("tensor_scalar_mul", [b_t1[:, :]],
                                  [b_i2[:, :], bigc],
                                  b_t1[:, :], b_i2[:, :], bigc)
                        s_ve.inc(i, 1)
                        i = VE.op("tensor_sub", [b_ltm[:, :]], [b_lt[:, :], b_t1[:, :]],
                                  b_ltm[:, :], b_lt[:, :], b_t1[:, :])
                        s_ve.inc(i, 1)
                        i = VE.op("reduce_max", [b_m1[:, 2:3]], [b_ltm[:, :]],
                                  b_m1[:, 2:3], b_ltm[:, :], mybir.AxisListType.X)
                        s_ve.inc(i, 1)
                        i = VE.op("tensor_single_scalar", [b_i2[:, :]],
                                  [b_lt[:, :], b_m1[:, 2:3]],
                                  b_i2[:, :], b_lt[:, :], b_m1[:, 2:3], ALU.is_ge)
                        s_ve.inc(i, 1)
                        i = VE.op("scalar_tensor_tensor", [b_w16[:, :]],
                                  [b_w16[:, :], b_rec[:, 0:1], b_i2[:, :]],
                                  b_w16[:, :], b_w16[:, :], b_rec[:, 0:1],
                                  b_i2[:, :], ALU.mult, ALU.mult)
                        s_ve.inc(i, 1)
                        PE.wait(s_ve)
                        bank_wait(3)
                        i = PE.op("transpose", [pk[3][0:NST, 0:128]],
                                  [b_w16[:, :], b_ident[:, :]],
                                  pk[3][0:NST, 0:128], b_w16[:, :],
                                  b_ident[:, :])
                        s_pe.inc(i, 1)
                        SE.wait(s_pe)
                        i = SE.op("copy",
                                  [b_mem[:, j * TC + tt2 * 128: j * TC + (tt2 + 1) * 128]],
                                  [pk[3][0:NST, 0:128]],
                                  b_mem[:, j * TC + tt2 * 128: j * TC + (tt2 + 1) * 128],
                                  pk[3][0:NST, 0:128])
                        s_se.inc(i, 1)
                        bank_release(3, s_se)
                for pre in range(2):
                    ge = ch * E + pre
                    SY.wait_v(s_w1f, max(0, ge - 2))
                    sload(st_w1, ge, b_w1s[:, ge % 3, :], W1[:, pre, :])
                    SY.wait_v(s_w2f, max(0, ge - 2))
                    sload(st_w2, ge, b_w2s[:, ge % 3, :, :], W2[:, pre, :, :])

            def hmid_phase(ch, idx, use4=False):
                e, j = divmod(idx, NH)
                ge = ch * E + e
                slot = ge % 3
                par = idx % 2
                if j == 0:
                    swait(PE, st_w1, ge)
                    swait(PE, st_w2, ge)
                rot = (0, 1, 2, 6) if use4 else (0, 1, 2)
                for it in range(4):
                    bki = rot[(hmq[0]) % len(rot)]
                    hmq[0] += 1
                    bank_wait(bki)
                    i = PE.op("matmul", [pk[bki][:, 0:TC]],
                              [b_w1s[:, slot, it * 128:(it + 1) * 128],
                               b_hf[:, j, :]],
                              pk[bki][:, 0:TC],
                              b_w1s[:, slot, it * 128:(it + 1) * 128],
                              b_hf[:, j, :], start=True, stop=True)
                    s_pe.inc(i, 1)
                    SE.wait(s_pe)
                    i = SE.op("activation", [b_hm[:, par, it, :]],
                              [pk[bki][:, 0:TC], b_b1[:, e, it:it + 1]],
                              b_hm[:, par, it, :], pk[bki][:, 0:TC], AF.Gelu,
                              bias=b_b1[:, e, it:it + 1])
                    s_se.inc(i, 1)
                    bank_release(bki, s_se)
                hm4_mark[idx] = s_se.n
                if j == NH - 1:
                    ne = e + 2
                    if ne < E:
                        ng = ch * E + ne
                        SY.wait_v(s_w1f, ge)
                        sload(st_w1, ng, b_w1s[:, ng % 3, :], W1[:, ne, :])
                        SY.wait_v(s_w2f, ge)
                        sload(st_w2, ng, b_w2s[:, ng % 3, :, :], W2[:, ne, :, :])

            def eo_phase(ch, idx):
                e2, j2 = divmod(idx, NH)
                ge2 = ch * E + e2
                slot2 = ge2 % 3
                par = idx % 2
                par4 = idx % 4
                PE.wait_v(s_se, hm4_mark[idx])
                bank_wait(7)
                for it in range(4):
                    i = PE.op("matmul", [pk[7][:, 0:TC]],
                              [b_w2s[:, slot2, it, :], b_hm[:, par, it, :]],
                              pk[7][:, 0:TC], b_w2s[:, slot2, it, :],
                              b_hm[:, par, it, :],
                              start=(it == 0), stop=(it == 3))
                    if j2 == 0 and it == 1 and ge2 >= 1:
                        s_w1f.inc(i, 1)
                    elif j2 == 0 and it == 2 and ge2 >= 1:
                        s_w2f.inc(i, 1)
                    else:
                        s_pe.inc(i, 1)
                SE.wait(s_pe)
                SE.wait_v(s_ve, eob_mark[par4])
                eob = b_eob[:, par4, :]
                i = SE.op("activation", [eob],
                          [pk[7][:, 0:TC], b_b2[:, e2:e2 + 1]],
                          eob, pk[7][:, 0:TC], AF.Identity,
                          bias=b_b2[:, e2:e2 + 1])
                s_se.inc(i, 1)
                bank_release(7, s_se)
                bank_wait(3)
                i = PE.op("matmul", [pk[3][:, 0:TC]],
                          [b_ident[:, :], b_mem[:, :]],
                          pk[3][:, 0:TC],
                          b_ident[0:NST, e2:e2 + 1].broadcast_to([NST, 128]),
                          b_mem[:, j2 * TC:(j2 + 1) * TC],
                          start=True, stop=True)
                s_pe.inc(i, 1)
                SE.wait(s_pe)
                SE.wait_v(s_ve, mbc_mark[par4])
                i = SE.op("copy", [b_mbc[:, par4, :]], [pk[3][:, 0:TC]],
                          b_mbc[:, par4, :], pk[3][:, 0:TC])
                s_se.inc(i, 1)
                bank_release(3, s_se)
                VE.wait(s_se)
                if idx < NH:
                    i = VE.op("tensor_tensor", [b_cmA[:, j2, :]],
                              [eob, b_mbc[:, par4, :]],
                              b_cmA[:, j2, :], eob, b_mbc[:, par4, :], ALU.mult)
                    s_ve.inc(i, 1)
                    eob_mark[par4] = s_ve.n
                    mbc_mark[par4] = s_ve.n
                else:
                    i = VE.op("tensor_tensor", [b_gt[:, :]],
                              [eob, b_mbc[:, par4, :]],
                              b_gt[:, :], eob, b_mbc[:, par4, :], ALU.mult)
                    s_ve.inc(i, 1)
                    eob_mark[par4] = s_ve.n
                    mbc_mark[par4] = s_ve.n
                    csrc, cdst = (b_cmA, b_cmB) if e2 % 2 == 1 else (b_cmB, b_cmA)
                    i = VE.op("tensor_add", [cdst[:, j2, :]],
                              [csrc[:, j2, :], b_gt[:, :]],
                              cdst[:, j2, :], csrc[:, j2, :], b_gt[:, :])
                    s_ve.inc(i, 1)

            def seg_wo(ch):
                comb = b_cmB
                SE.wait(s_ve)
                for m in range(8):
                    bk = pk[m % 2]
                    if m == 0:
                        PE.wait(s_ve)
                    bank_wait(m % 2)
                    for kc in range(NH):
                        i = PE.op("matmul", [bk[:, 0:TC]],
                                  [b_wo[:, kc, m * 128:(m + 1) * 128], comb[:, kc, :]],
                                  bk[:, 0:TC],
                                  b_wo[:, kc, m * 128:(m + 1) * 128],
                                  comb[:, kc, :], start=(kc == 0), stop=False)
                        s_pe.inc(i, 1)
                    i = PE.op("matmul", [bk[:, 0:TC]],
                              [b_bolh[:, m * 128:(m + 1) * 128], b_onesr[:, :]],
                              bk[:, 0:TC], b_bolh[0:1, m * 128:(m + 1) * 128],
                              b_onesr[:, :], start=False, stop=True)
                    s_pe.inc(i, 1)
                    SE.wait(s_pe)
                    i = SE.op("copy", [b_xc[:, m, :]], [bk[:, 0:TC]],
                              b_xc[:, m, :], bk[:, 0:TC])
                    s_se.inc(i, 1)
                    bank_release(m % 2, s_se)
                SE.wait(s_gp)
                for tt in range(NTT):
                    for d in range(NMT):
                        bki = 6 + (d % 2)
                        bk = pk[bki]
                        if tt == 0 and d == 0:
                            PE.wait(s_se)
                        bank_wait(bki)
                        i = PE.op("transpose", [bk[:, 0:128]],
                                  [b_xc[:, d, :], b_ident[:, :]],
                                  bk[:, 0:128],
                                  b_xc[:, d, tt * 128:(tt + 1) * 128],
                                  b_ident[:, :])
                        s_pe.inc(i, 1)
                        SE.wait(s_pe)
                        i = SE.op("copy", [b_yB[:, 0, :]], [bk[:, 0:128]],
                                  bass.AP(b_yB, tt * DM + d * 128,
                                          [[8 * TC, 128], [1, 128]]),
                                  bk[:, 0:128])
                        s_se.inc(i, 1)
                        bank_release(bki, s_se)
                mark_outtr[0] = s_pe.n
                SY.wait(s_se)
                dma(ccfi[:, :], b_yB[:, :, :], s_cci)
                GP.wait(s_cci)
                i = None
                if GP.e is not None:
                    i = GP.e.collective_compute(
                        "AllReduce", ALU.add, replica_groups=groups,
                        ins=[ccfi[:, :]], outs=[ccfo[:, :]])
                s_cc.inc(i, 1)
                SY.wait(s_cc)
                dma(b_yB[:, :, :], ccfo[:, :], s_cco)

            def seg_out(ch):
                t0 = ch * TC
                x2_flat = bass.AP(b_dt, 0, [[8 * TC, 128], [1, 4 * TC]])
                l5_flat = bass.AP(b_dt, 4 * TC, [[8 * TC, 128], [1, 4 * TC]])
                for tt in range(NTT):
                    ftm = bass.AP(b_yB, tt * DM, [[8 * TC, 128], [1, DM]])
                    VE.wait(s_cco)
                    layernorm_tm(ftm, l5_flat)
                    VE.wait(s_out)
                    i = VE.op("tensor_add", [b_dt[:, 0, :]],
                              [b_dt[:, 4, :], b_x1[:, tt, :]],
                              x2_flat, l5_flat, b_x1[:, tt, :])
                    s_ve.inc(i, 1)
                    SY.wait(s_ve)
                    i = None
                    if SY.e is not None:
                        i = SY.e.dma_start(
                            OUT[t0 + tt * 128: t0 + (tt + 1) * 128, :],
                            x2_flat)
                    s_out.inc(i, 16)

            scan_state = {"marks": []}
            mark_s1t = [0]
            mark_win = [0]
            mark_fin = [0]
            mark_outtr = [0]
            PRE0 = True
            for i_it in range(NCH + 2):
                och = i_it - 2
                fch = i_it - 1
                mch = i_it
                if och >= 0:
                    seg_out(och)
                if 0 <= fch < NCH:
                    seg_sandwich(fch)
                    seg_wm_router(fch)
                if mch < NCH:
                    seg_loads_ln(mch)
                if 0 <= fch < NCH:
                    eix = [0]

                    def expert_step(k):
                        if eix[0] < E * NH:
                            hmid_phase(fch, eix[0], use4=(k >= 28))
                            if eix[0] >= 1:
                                eo_phase(fch, eix[0] - 1)
                            eix[0] += 1

                    for idx in range(E * NH):
                        expert_step(idx)
                        if idx in (29, 33, 37, 41, 45, 49, 53):
                            expert_step(idx)
                        if mch < NCH:
                            if idx < 16:
                                seg_win_mm(mch, idx)
                            if 1 <= idx < 17:
                                seg_win_post(mch, idx - 1)
                            elif 18 <= idx < 26:
                                seg_dt(mch, idx - 18)
                            elif 26 <= idx < 58 and (idx - 26) % 2 == 0:
                                seg_scan_n(mch, (idx - 26) // 2)
                    eo_phase(fch, E * NH - 1)
                elif mch < NCH:
                    for j in range(16):
                        seg_win_mm(mch, j)
                        seg_win_post(mch, j)
                    for d in range(8):
                        seg_dt(mch, d)
                    for n in range(NST):
                        seg_scan_n(mch, n)
                if 0 <= fch < NCH:
                    seg_wo(fch)
                if mch < NCH:
                    seg_wout(mch)
                PRE0 = False

            if SY.e is not None:
                SY.e.wait_ge(s_out.h, s_out.n)

            for s in (s_ld, s_wf, s_w1f, s_w2f, s_cci, s_cc, s_cco, s_pe,
                      s_ve, s_se, s_out, s_xf, s_wof, s_wmf, s_gp,
                      *st_x, *st_win, *st_wo, *st_wm, *st_w1, *st_w2):
                s.n = 0

        @block.sync
        def _(sync):
            gen("sync", sync, None, None, None, None)

        @block.tensor
        def _(tensor):
            gen("tensor", None, tensor, None, None, None)

        @block.vector
        def _(vector):
            gen("vector", None, None, vector, None, None)

        @block.scalar
        def _(scalar):
            gen("scalar", None, None, None, scalar, None)

        @block.gpsimd
        def _(gpsimd):
            gen("gpsimd", None, None, None, None, gpsimd)

    return nc


def prep_in_maps(inputs, n_cores=8, T=4096):
    g = {k: np.asarray(v, dtype=np.float32) for k, v in inputs.items()}
    x = g["L0_attn_x"]
    W_in = g["L0_attn_W_in"]
    conv_w = g["L0_attn_conv_w"]
    conv_b = g["L0_attn_conv_b"]
    W_xp = g["L0_attn_W_xp"]
    W_dt = g["L0_attn_W_dt"]
    b_dt = g["L0_attn_b_dt"]
    A_log = g["L0_attn_A_log"]
    Dv = g["L0_attn_D"]
    W_out = g["L0_attn_W_out"]
    ln_g = g["L0_attn_ln_g"]
    ln_b = g["L0_attn_ln_b"]
    Wm = g["L0_ffn_Wm"]
    bm = g["L0_ffn_bm"]
    emb = g["L0_ffn_expert_emb"]
    W1 = g["L0_ffn_W1"]
    b1 = g["L0_ffn_b1"]
    W2 = g["L0_ffn_W2"]
    b2 = g["L0_ffn_b2"]
    Wo = g["L0_ffn_Wo"]
    bo = g["L0_ffn_bo"]

    consts = np.zeros((128, 8), np.float32)
    consts[:, C_ONE] = 1.0
    consts[:, C_NEG1] = -1.0
    consts[:, C_EPS] = 1e-5
    consts[:, C_BIG] = 1e30

    def tile_cols(v, nt):
        return np.ascontiguousarray(v.reshape(nt, 128).T)

    in_maps = []
    for c in range(n_cores):
        b = c // 2
        q = c % 2
        ch0 = q * 1024
        Wc = np.concatenate([W_in[:, ch0:ch0 + 1024],
                             W_in[:, 2048 + ch0:2048 + ch0 + 1024]], axis=1)
        win = Wc.reshape(8, 128, 16, 128)
        win = np.ascontiguousarray(win.transpose(2, 1, 0, 3))
        wxp = np.ascontiguousarray(
            W_xp[ch0:ch0 + 1024].reshape(8, 128, 33).transpose(1, 0, 2))
        alog = np.ascontiguousarray(
            A_log[ch0:ch0 + 1024].reshape(8, 128, 16).transpose(1, 0, 2))
        convw = np.ascontiguousarray(
            conv_w[ch0:ch0 + 1024].reshape(8, 128, 4).transpose(1, 0, 2))
        wout = np.ascontiguousarray(
            W_out[ch0:ch0 + 1024].reshape(8, 128, 1024).transpose(1, 0, 2))
        h0 = q * 512
        wm = np.ascontiguousarray(
            Wm[:, h0:h0 + 512].reshape(8, 128, 512).transpose(1, 0, 2))
        wo = np.ascontiguousarray(
            Wo[h0:h0 + 512].reshape(4, 128, 1024).transpose(1, 0, 2))
        w1 = np.ascontiguousarray(W1.transpose(1, 0, 2))
        w2 = np.ascontiguousarray(
            W2.reshape(16, 4, 128, 128).transpose(2, 0, 1, 3))
        b1t = np.ascontiguousarray(b1.reshape(16, 4, 128).transpose(2, 0, 1))
        b2t = np.ascontiguousarray(b2.T)
        bolh = (bo if q == 0 else np.zeros_like(bo)).reshape(1, 1024)
        m = {
            "X": np.ascontiguousarray(x[b, :T]),
            "WIN": win,
            "LNG": tile_cols(ln_g, 8),
            "LNB": tile_cols(ln_b, 8),
            "CONVW": convw,
            "CONVB": tile_cols(conv_b[ch0:ch0 + 1024], 8),
            "WXP": wxp,
            "WDT": tile_cols(-W_dt[0, ch0:ch0 + 1024], 8),
            "BDT": tile_cols(-b_dt[ch0:ch0 + 1024], 8),
            "ALOG": alog,
            "DDIN": tile_cols(-Dv[ch0:ch0 + 1024], 8),
            "WOUT": wout,
            "WM": wm,
            "BM": tile_cols(bm[h0:h0 + 512], 4),
            "EMB": emb,
            "W1": w1,
            "B1": b1t,
            "W2": w2,
            "B2": b2t,
            "WO": wo,
            "BOLH": np.ascontiguousarray(bolh),
            "ONES1": np.ones((1, 128), np.float32),
            "ONESR": np.ones((1, TC), np.float32),
            "IDENT": np.eye(128, dtype=np.float32),
            "CONSTS": consts,
        }
        in_maps.append(m)
    return in_maps


def kernel(**inputs):
    n_cores = 8
    in_maps = prep_in_maps(inputs, n_cores=n_cores)
    nc = build_nc(T=4096, n_cores=n_cores)
    res = run_bass_kernel_spmd(nc, in_maps, core_ids=list(range(n_cores)))
    out = np.stack([np.asarray(res.results[2 * b]["OUT"]) for b in range(4)])
    return out


# version 26
# speedup vs baseline: 1.0043x; 1.0043x over previous
import numpy as np
import concourse.bass as bass
import concourse.mybir as mybir
from concourse.bass_utils import run_bass_kernel_spmd

F32 = mybir.dt.float32
AF = mybir.ActivationFunctionType
ALU = mybir.AluOpType

DM = 1024
DI_H = 1024
NST = 16
TC = 256
NTT = TC // 128
NDT = DI_H // 128
NMT = DM // 128
E = 16
HDD = 128
IFF = 512
NH = 4

C_ONE, C_NEG1, C_EPS, C_BIG, C_ZERO = 0, 1, 2, 3, 4


class Sem:
    def __init__(self, h):
        self.h = h
        self.n = 0

    def inc(self, ins, k):
        if ins is not None:
            ins.then_inc(self.h, k)
        self.n += k


class Eng:
    def __init__(self, eng, active, track=True, self_sem=None):
        self.e = eng if active else None
        self.dirty = set()
        self.reads = set()
        self.track = track
        self.self_sem = self_sem

    def wait(self, sem):
        if self.e is not None and sem.n > 0:
            self.e.wait_ge(sem.h, sem.n)

    def wait_v(self, sem, v):
        if self.e is not None and v > 0:
            self.e.wait_ge(sem.h, v)

    def free(self, sem):
        i = None
        if self.e is not None:
            self.e.drain()
            i = self.e.nop()
        sem.inc(i, 1)

    def op(self, method, outs, ins, *args, **kw):
        if self.e is None:
            return None
        def nm(lst):
            out = set()
            for a in lst:
                try:
                    out.add(a.name)
                except Exception:
                    pass
            return out
        if self.track:
            no, ni = nm(outs), nm(ins)
            if (no & (self.dirty | self.reads)) or (ni & self.dirty):
                if self.self_sem is not None:
                    self.e.wait_ge(self.self_sem.h, self.self_sem.n)
                else:
                    self.e.drain()
                self.dirty.clear()
                self.reads.clear()
            ins_obj = getattr(self.e, method)(*args, **kw)
            self.dirty |= no
            self.reads |= ni
            return ins_obj
        return getattr(self.e, method)(*args, **kw)


def build_nc(T=4096, n_cores=8):
    NCH = T // TC
    groups = [[2 * i, 2 * i + 1] for i in range(n_cores // 2)]
    nc = bass.Bass("TRN2", target_bir_lowering=False, debug=False)

    X = nc.dram_tensor("X", [T, DM], F32, kind="ExternalInput")
    WIN = nc.dram_tensor("WIN", [16, 128, 8, 128], F32, kind="ExternalInput")
    LNG = nc.dram_tensor("LNG", [128, 8], F32, kind="ExternalInput")
    LNB = nc.dram_tensor("LNB", [128, 8], F32, kind="ExternalInput")
    CONVW = nc.dram_tensor("CONVW", [128, 8, 4], F32, kind="ExternalInput")
    CONVB = nc.dram_tensor("CONVB", [128, 8], F32, kind="ExternalInput")
    WXP = nc.dram_tensor("WXP", [128, 8, 33], F32, kind="ExternalInput")
    WDT = nc.dram_tensor("WDT", [128, 8], F32, kind="ExternalInput")
    BDT = nc.dram_tensor("BDT", [128, 8], F32, kind="ExternalInput")
    ALOG = nc.dram_tensor("ALOG", [128, 8, NST], F32, kind="ExternalInput")
    DDIN = nc.dram_tensor("DDIN", [128, 8], F32, kind="ExternalInput")
    WOUT = nc.dram_tensor("WOUT", [128, 8, DM], F32, kind="ExternalInput")
    WM = nc.dram_tensor("WM", [128, 8, NH * 128], F32, kind="ExternalInput")
    BM = nc.dram_tensor("BM", [128, NH], F32, kind="ExternalInput")
    EMB = nc.dram_tensor("EMB", [128, NST], F32, kind="ExternalInput")
    W1 = nc.dram_tensor("W1", [128, E, IFF], F32, kind="ExternalInput")
    B1 = nc.dram_tensor("B1", [128, E, 4], F32, kind="ExternalInput")
    W2 = nc.dram_tensor("W2", [128, E, 4, 128], F32, kind="ExternalInput")
    B2 = nc.dram_tensor("B2", [128, E], F32, kind="ExternalInput")
    WO = nc.dram_tensor("WO", [128, NH, DM], F32, kind="ExternalInput")
    BOLH = nc.dram_tensor("BOLH", [1, 8 * 128], F32, kind="ExternalInput")
    ONES1 = nc.dram_tensor("ONES1", [1, 128], F32, kind="ExternalInput")
    ONESR = nc.dram_tensor("ONESR", [1, TC], F32, kind="ExternalInput")
    IDENT = nc.dram_tensor("IDENT", [128, 128], F32, kind="ExternalInput")
    CONSTS = nc.dram_tensor("CONSTS", [128, 8], F32, kind="ExternalInput")
    OUT = nc.dram_tensor("OUT", [T, DM], F32, kind="ExternalOutput")
    ccxi = nc.dram_tensor("ccxi", [33, TC], F32)
    ccxo = nc.dram_tensor("ccxo", [33, TC], F32)
    ccmi = nc.dram_tensor("ccmi", [128, 8 * TC], F32)
    ccmo = nc.dram_tensor("ccmo", [128, 8 * TC], F32)
    ccfi = nc.dram_tensor("ccfi", [128, 8 * TC], F32)
    ccfo = nc.dram_tensor("ccfo", [128, 8 * TC], F32)

    from contextlib import ExitStack
    es = ExitStack()
    SB = lambda n_, s: es.enter_context(nc.sbuf_tensor(n_, s, F32))
    PS = lambda n_: es.enter_context(nc.psum_tensor(n_, [128, 512], F32))
    SEM = lambda n_: Sem(es.enter_context(nc.semaphore(n_)))

    with es:
        b_wos = SB("b_wos", [128, 2, DM])
        b_wms = SB("b_wms", [128, 3, NH * 128])
        b_wo = SB("b_wo", [128, NH, DM])
        b_wxp = SB("b_wxp", [128, 8, 33])
        b_A = SB("b_A", [128, 8, NST])
        b_convw = SB("b_convw", [128, 8, 4])
        b_convb = SB("b_convb", [128, 8])
        b_wdt = SB("b_wdt", [128, 8])
        b_bdt = SB("b_bdt", [128, 8])
        b_dd = SB("b_dd", [128, 8])
        b_lng = SB("b_lng", [128, 8])
        b_lnb = SB("b_lnb", [128, 8])
        b_bm = SB("b_bm", [128, NH])
        b_emb = SB("b_emb", [128, NST])
        b_b1 = SB("b_b1", [128, E, 4])
        b_b2 = SB("b_b2", [128, E])
        b_bolh = SB("b_bolh", [1, 8 * 128])
        b_ones1 = SB("b_ones1", [1, 128])
        b_onesr = SB("b_onesr", [1, TC])
        b_ident = SB("b_ident", [128, 128])
        b_c = SB("b_c", [128, 8])
        b_bvec = SB("b_bvec", [128, 16])
        b_win = SB("b_win", [128, 2, 8, 128])
        b_w1s = SB("b_w1s", [128, 3, IFF])
        b_w2s = SB("b_w2s", [128, 3, 4, 128])
        b_x0 = SB("b_x0", [128, NTT, DM])
        b_xn = SB("b_xn", [128, NTT, DM])
        b_tma = SB("b_tma", [128, NTT, DM])
        b_x1 = SB("b_x1", [128, NTT, DM])
        b_s1T = SB("b_s1T", [128, 8, TC])
        b_zs = SB("b_zs", [128, 8, TC])
        b_xc = SB("b_xc", [128, 8, TC])
        b_dt = SB("b_dt", [128, 8, TC])
        b_yA = SB("b_yA", [128, 8, TC])
        b_yB = SB("b_yB", [128, 8, TC])
        b_po = SB("b_po", [128, 8, TC])
        b_hf = SB("b_hf", [128, NH, TC])
        b_hm = SB("b_hm", [128, 2, 4, TC])
        b_cmA = SB("b_cmA", [128, NH, TC])
        b_cmB = SB("b_cmB", [128, NH, TC])
        b_eob = SB("b_eob", [128, 4, TC])
        b_tmp = SB("b_tmp", [128, TC])
        b_gt = SB("b_gt", [128, TC])
        b_mbc = SB("b_mbc", [128, 4, TC])
        b_da = SB("b_da", [128, 8, TC])
        b_dbx = SB("b_dbx", [128, 2, 4, TC])
        b_xhe = SB("b_xhe", [128, TC + 8])
        b_cv0 = SB("b_cv0", [128, TC])
        b_cv1 = SB("b_cv1", [128, TC])
        b_ubc = SB("b_ubc", [128, TC])
        b_tail = SB("b_tail", [128, 8, 3])
        b_state = SB("b_state", [128, 128])
        b_xpj = SB("b_xpj", [33, TC])
        b_lg = SB("b_lg", [16, TC])
        b_mem = SB("b_mem", [16, NH * TC])
        b_lt = SB("b_lt", [128, NST])
        b_w16 = SB("b_w16", [128, NST])
        b_i2 = SB("b_i2", [128, NST])
        b_t1 = SB("b_t1", [128, NST])
        b_ltm = SB("b_ltm", [128, NST])
        b_st = SB("b_st", [128, 12])
        b_mv = SB("b_mv", [128, 8])
        b_rs = SB("b_rs", [128, 8])
        b_m1 = SB("b_m1", [128, 4])
        b_rec = SB("b_rec", [128, 1])
        pk = [PS(f"pk{i}") for i in range(8)]

        s_ld = SEM("s_ld")
        st_x = (SEM("s_x0a"), SEM("s_x0b"))
        st_win = (SEM("s_wina"), SEM("s_winb"))
        st_wo = (SEM("s_woa"), SEM("s_wob"))
        st_wm = (SEM("s_wma"), SEM("s_wmb"))
        st_w1 = (SEM("s_w1a"), SEM("s_w1b"))
        st_w2 = (SEM("s_w2a"), SEM("s_w2b"))
        s_wf = SEM("s_wf")
        s_wof = SEM("s_wof")
        s_wmf = SEM("s_wmf")
        s_w1f = SEM("s_w1f")
        s_w2f = SEM("s_w2f")
        s_cci = SEM("s_cci")
        s_cc = SEM("s_cc")
        s_cco = SEM("s_cco")
        s_pe = SEM("s_pe")
        s_ve = SEM("s_ve")
        s_se = SEM("s_se")
        s_out = SEM("s_out")
        s_gp = SEM("s_gp")
        s_xf = SEM("s_xf")

        block = es.enter_context(nc.Block())

        def gen(active, sync_e, tensor_e, vector_e, scalar_e, gpsimd_e):
            SY = Eng(sync_e, active == "sync")
            PE = Eng(tensor_e, active == "tensor", track=False)
            VE = Eng(vector_e, active == "vector", self_sem=s_ve)
            SE = Eng(scalar_e, active == "scalar", self_sem=s_se)
            GP = Eng(gpsimd_e, active == "gpsimd")

            ones = b_c[:, C_ONE:C_ONE + 1]
            neg1 = b_c[:, C_NEG1:C_NEG1 + 1]
            epsc = b_c[:, C_EPS:C_EPS + 1]
            bigc = b_c[:, C_BIG:C_BIG + 1]

            def sload(pair, k, dst, src):
                dma(dst, src, pair[k % 2])

            def swait(E, pair, k):
                E.wait_v(pair[k % 2], 16 * (k // 2 + 1))

            def dma(dst, src, sem):
                SY.wait(sem)
                i = None
                if SY.e is not None:
                    i = SY.e.dma_start(dst, src)
                sem.inc(i, 16)

            dma(b_wo[:, :, :], WO[:, :, :], s_ld)
            dma(b_wxp[:, :, :], WXP[:, :, :], s_ld)
            dma(b_A[:, :, :], ALOG[:, :, :], s_ld)
            dma(b_convw[:, :, :], CONVW[:, :, :], s_ld)
            dma(b_convb[:, :], CONVB[:, :], s_ld)
            dma(b_wdt[:, :], WDT[:, :], s_ld)
            dma(b_bdt[:, :], BDT[:, :], s_ld)
            dma(b_dd[:, :], DDIN[:, :], s_ld)
            dma(b_lng[:, :], LNG[:, :], s_ld)
            dma(b_lnb[:, :], LNB[:, :], s_ld)
            dma(b_bm[:, :], BM[:, :], s_ld)
            dma(b_emb[:, :], EMB[:, :], s_ld)
            dma(b_b1[:, :, :], B1[:, :, :], s_ld)
            dma(b_b2[:, :], B2[:, :], s_ld)
            dma(b_bolh[:, :], BOLH[:, :], s_ld)
            dma(b_ones1[:, :], ONES1[:, :], s_ld)
            dma(b_onesr[:, :], ONESR[:, :], s_ld)
            dma(b_ident[:, :], IDENT[:, :], s_ld)
            dma(b_c[:, :], CONSTS[:, :], s_ld)
            n_pre = s_ld.n

            SE.wait(s_ld)
            VE.wait(s_ld)
            for d in range(8):
                SE.wait(s_ve)
                i = SE.op("activation", [b_cv0[:, 0:NST]], [b_A[:, d, :]],
                          b_cv0[:, 0:NST], b_A[:, d, :], AF.Exp)
                s_se.inc(i, 1)
                VE.wait(s_se)
                i = VE.op("tensor_copy", [b_A[:, d, :]], [b_cv0[:, 0:NST]],
                          b_A[:, d, :], b_cv0[:, 0:NST])
                s_ve.inc(i, 1)
            i = VE.op("memset", [b_tail[:, :, :]], [], b_tail[:, :, :], 0.0)
            s_ve.inc(i, 1)
            i = VE.op("memset", [b_state[:, :]], [], b_state[:, :], 0.0)
            s_ve.inc(i, 1)

            PE.wait(s_ld)
            GP.wait(s_ld)
            sload(st_win, 0, b_win[:, 0, :, :], WIN[0, :, :, :])
            sload(st_win, 1, b_win[:, 1, :, :], WIN[1, :, :, :])

            out_dmas = 0
            tma_mark = [0]
            eob_mark = [0, 0, 0, 0]
            mbc_mark = [0, 0, 0, 0]
            bmark = {}

            def bank_wait(i):
                m = bmark.get(i)
                if m is not None:
                    PE.wait_v(m[0], m[1])

            def bank_release(i, sem):
                bmark[i] = (sem, sem.n)

            ln_rot = [0]

            def layernorm_tm(src_tile, dst_tile, pe_mark=None):
                sl = (ln_rot[0] % 4) * 2
                ln_rot[0] += 1
                mv = b_mv[:, sl:sl + 2]
                i = VE.op("bn_stats", [b_st[:, 0:6]], [src_tile[:, 0:512]],
                          b_st[:, 0:6], src_tile[:, 0:512])
                s_ve.inc(i, 1)
                i = VE.op("bn_stats", [b_st[:, 6:12]], [src_tile[:, 512:1024]],
                          b_st[:, 6:12], src_tile[:, 512:1024])
                s_ve.inc(i, 1)
                i = VE.op("bn_aggr", [b_mv[:, 0:2]], [b_st[:, :]],
                          mv, b_st[:, :])
                s_ve.inc(i, 1)
                SE.wait(s_ve)
                i = SE.op("activation", [b_rs[:, 1:2]], [b_mv[:, 0:2], epsc],
                          b_rs[:, sl + 1:sl + 2], mv[:, 1:2], AF.Sqrt,
                          bias=epsc)
                s_se.inc(i, 1)
                VE.wait(s_se)
                i = VE.op("reciprocal", [b_rs[:, 0:1]], [b_rs[:, 1:2]],
                          b_rs[:, sl:sl + 1], b_rs[:, sl + 1:sl + 2])
                s_ve.inc(i, 1)
                if pe_mark is not None:
                    VE.wait_v(s_pe, pe_mark)
                i = VE.op("tensor_scalar", [dst_tile],
                          [src_tile, b_mv[:, 0:2], b_rs[:, 0:1]],
                          dst_tile, src_tile, mv[:, 0:1], b_rs[:, sl:sl + 1],
                          ALU.subtract, ALU.mult)
                s_ve.inc(i, 1)

            hm4_mark = {}
            hmq = [0]
            scan_ue_mark = [0]

            def seg_loads_ln(ch):
                if ch > 0:
                    SY.wait(s_xf)
                    SE.wait(s_out)
                    VE.wait(s_out)
                t0 = ch * TC
                for tt in range(NTT):
                    sload(st_x, ch * NTT + tt, b_x0[:, tt, :],
                          X[t0 + tt * 128: t0 + (tt + 1) * 128, :])
                VE.wait(s_ld)
                SE.wait(s_ld)
                for tt in range(NTT):
                    swait(VE, st_x, ch * NTT + tt)
                    swait(SE, st_x, ch * NTT + tt)
                    layernorm_tm(b_x0[:, tt, :], b_xn[:, tt, :])
                    layernorm_tm(b_xn[:, tt, :], b_tma[:, tt, :],
                                 pe_mark=tma_mark[0])
                    for d in range(NMT):
                        bki = 6 + (d % 2)
                        bk = pk[bki]
                        if d == 0:
                            PE.wait(s_ve)
                        bank_wait(bki)
                        i = PE.op("transpose", [bk[:, 0:128]],
                                  [b_tma[:, tt, :], b_ident[:, :]],
                                  bk[:, 0:128],
                                  b_tma[:, tt, d * 128:(d + 1) * 128],
                                  b_ident[:, :])
                        s_pe.inc(i, 1)
                        SE.wait(s_pe)
                        SE.wait_v(s_ve, scan_ue_mark[0])
                        i = SE.op("activation",
                                  [b_s1T[:, d, tt * 128:(tt + 1) * 128]],
                                  [bk[:, 0:128], b_lng[:, d:d + 1]],
                                  b_s1T[:, d, tt * 128:(tt + 1) * 128],
                                  bk[:, 0:128], AF.Identity,
                                  scale=b_lng[:, d:d + 1])
                        s_se.inc(i, 1)
                        bank_release(bki, s_se)
                    tma_mark[0] = s_pe.n
                mark_s1t[0] = s_se.n

            def seg_win_mm(ch, j):
                g = ch * 16 + j
                slot = g % 2
                bki = 4 + (j % 2)
                bk = pk[bki]
                swait(PE, st_win, g)
                bank_wait(bki)
                if j == 0:
                    PE.wait_v(s_se, mark_s1t[0])
                for kc in range(8):
                    i = PE.op("matmul", [bk[:, 0:TC]],
                              [b_win[:, slot, kc, :], b_s1T[:, kc, :]],
                              bk[:, 0:TC], b_win[:, slot, kc, :],
                              b_s1T[:, kc, :],
                              start=(kc == 0), stop=(kc == 7))
                    if kc == 3 and g >= 1:
                        s_wf.inc(i, 1)
                    else:
                        s_pe.inc(i, 1)
                if ch == 0 and PRE0:
                    bank_wait(7)
                    for kc in range(8):
                        i = PE.op("matmul", [pk[7][:, 0:1]],
                                  [b_win[:, slot, kc, :], b_lnb[:, kc:kc + 1]],
                                  pk[7][:, 0:1], b_win[:, slot, kc, :],
                                  b_lnb[:, kc:kc + 1],
                                  start=(kc == 0), stop=(kc == 7))
                        s_pe.inc(i, 1)
                    SE.wait(s_pe)
                    i = SE.op("copy", [b_bvec[:, j:j + 1]], [pk[7][:, 0:1]],
                              b_bvec[:, j:j + 1], pk[7][:, 0:1])
                    s_se.inc(i, 1)
                    bank_release(7, s_se)
                nj = g + 1
                if g > 0 and nj < NCH * 16:
                    SY.wait_v(s_wf, g)
                    sload(st_win, nj, b_win[:, nj % 2, :, :], WIN[nj % 16, :, :, :])

            def seg_win_post(ch, j):
                g = ch * 16 + j
                bki = 4 + (j % 2)
                bk = pk[bki]
                SE.wait(s_pe)
                if j < 8:
                    d = j
                    i = SE.op("activation", [b_xhe[:, 3:3 + TC]],
                              [bk[:, 0:TC], b_bvec[:, j:j + 1]],
                              b_xhe[:, 3:3 + TC], bk[:, 0:TC], AF.Identity,
                              bias=b_bvec[:, j:j + 1])
                    s_se.inc(i, 1)
                    bank_release(bki, s_se)
                    VE.wait(s_se)
                    i = VE.op("tensor_copy", [b_xhe[:, 0:3]], [b_tail[:, d, :]],
                              b_xhe[:, 0:3], b_tail[:, d, :])
                    s_ve.inc(i, 1)
                    i = VE.op("tensor_copy", [b_tail[:, d, :]],
                              [b_xhe[:, TC:TC + 3]],
                              b_tail[:, d, :], b_xhe[:, TC:TC + 3])
                    s_ve.inc(i, 1)
                    i = VE.op("tensor_scalar", [b_cv0[:, :]],
                              [b_xhe[:, 0:TC], b_convw[:, d, 0:1], b_convb[:, d:d + 1]],
                              b_cv0[:, :], b_xhe[:, 0:TC], b_convw[:, d, 0:1],
                              b_convb[:, d:d + 1], ALU.mult, ALU.add)
                    s_ve.inc(i, 1)
                    i = VE.op("scalar_tensor_tensor", [b_cv1[:, :]],
                              [b_xhe[:, 1:1 + TC], b_convw[:, d, 1:2], b_cv0[:, :]],
                              b_cv1[:, :], b_xhe[:, 1:1 + TC],
                              b_convw[:, d, 1:2], b_cv0[:, :], ALU.mult, ALU.add)
                    s_ve.inc(i, 1)
                    i = VE.op("scalar_tensor_tensor", [b_tmp[:, :]],
                              [b_xhe[:, 2:2 + TC], b_convw[:, d, 2:3], b_cv1[:, :]],
                              b_tmp[:, :], b_xhe[:, 2:2 + TC],
                              b_convw[:, d, 2:3], b_cv1[:, :], ALU.mult, ALU.add)
                    s_ve.inc(i, 1)
                    i = VE.op("scalar_tensor_tensor", [b_cv0[:, :]],
                              [b_xhe[:, 3:3 + TC], b_convw[:, d, 3:4], b_tmp[:, :]],
                              b_cv0[:, :], b_xhe[:, 3:3 + TC],
                              b_convw[:, d, 3:4], b_tmp[:, :], ALU.mult, ALU.add)
                    s_ve.inc(i, 1)
                    SE.wait(s_ve)
                    if d == 0:
                        SE.wait_v(s_pe, mark_outtr[0])
                    i = SE.op("activation", [b_xc[:, d, :]], [b_cv0[:, :]],
                              b_xc[:, d, :], b_cv0[:, :], AF.Silu)
                    s_se.inc(i, 1)
                    VE.wait(s_se)
                    VE.wait(s_gp)
                    i = VE.op("tensor_scalar_mul", [b_yA[:, d, :]],
                              [b_xc[:, d, :], b_dd[:, d:d + 1]],
                              b_yA[:, d, :], b_xc[:, d, :], b_dd[:, d:d + 1])
                    s_ve.inc(i, 1)
                    PE.wait(s_se)
                    if d == 0:
                        bank_wait(6)
                    i = PE.op("matmul", [pk[6][0:33, 0:TC]],
                              [b_wxp[:, d, :], b_xc[:, d, :]],
                              pk[6][0:33, 0:TC], b_wxp[:, d, :], b_xc[:, d, :],
                              start=(d == 0), stop=(d == 7))
                    s_pe.inc(i, 1)
                    if d == 7:
                        VE.wait(s_pe)
                        i = VE.op("tensor_copy", [b_xpj[:, :]],
                                  [pk[6][0:33, 0:TC]],
                                  b_xpj[:, :], pk[6][0:33, 0:TC])
                        s_ve.inc(i, 1)
                        bank_release(6, s_ve)
                        SY.wait(s_ve)
                        dma(ccxi[:, :], b_xpj[:, :], s_cci)
                        GP.wait(s_cci)
                        i = None
                        if GP.e is not None:
                            i = GP.e.collective_compute(
                                "AllReduce", ALU.add, replica_groups=groups,
                                ins=[ccxi[:, :]], outs=[ccxo[:, :]])
                        s_cc.inc(i, 1)
                        SY.wait(s_cc)
                        dma(b_xpj[:, :], ccxo[:, :], s_cco)
                else:
                    d = j - 8
                    if d == 0:
                        SE.wait(s_ve)
                    i = SE.op("activation", [b_zs[:, d, :]],
                              [bk[:, 0:TC], b_bvec[:, j:j + 1]],
                              b_zs[:, d, :], bk[:, 0:TC], AF.Silu,
                              bias=b_bvec[:, j:j + 1])
                    s_se.inc(i, 1)
                    bank_release(bki, s_se)
                if j == 15:
                    mark_win[0] = s_pe.n

            def seg_dt(ch, d):
                if d == 0:
                    SE.wait(s_out)
                    PE.wait(s_cco)
                    bank_wait(6)
                    i = PE.op("matmul", [pk[6][:, 0:TC]],
                              [b_ident[:, :], b_xpj[:, :]],
                              pk[6][:, 0:TC],
                              b_ident[0:33, 32:33].broadcast_to([33, 128]),
                              b_xpj[:, :], start=True, stop=True)
                    s_pe.inc(i, 1)
                    SE.wait(s_pe)
                    i = SE.op("copy", [b_ubc[:, :]], [pk[6][:, 0:TC]],
                              b_ubc[:, :], pk[6][:, 0:TC])
                    s_se.inc(i, 1)
                    bank_release(6, s_se)
                SE.wait(s_ve)
                i = SE.op("activation", [b_cv0[:, :]],
                          [b_ubc[:, :], b_wdt[:, d:d + 1], b_bdt[:, d:d + 1]],
                          b_cv0[:, :], b_ubc[:, :], AF.Sigmoid,
                          scale=b_wdt[:, d:d + 1], bias=b_bdt[:, d:d + 1])
                s_se.inc(i, 1)
                i = SE.op("activation", [b_dt[:, d, :]], [b_cv0[:, :]],
                          b_dt[:, d, :], b_cv0[:, :], AF.Ln)
                s_se.inc(i, 1)
                VE.wait(s_se)
                if d == 0:
                    VE.wait_v(s_pe, mark_win[0])
                i = VE.op("tensor_mul", [b_s1T[:, d, :]], [b_dt[:, d, :], b_xc[:, d, :]],
                          b_s1T[:, d, :], b_dt[:, d, :], b_xc[:, d, :])
                s_ve.inc(i, 1)

            def seg_scan_n(ch, n):
                bkn = 4 + (n % 2)
                bank_wait(bkn)
                if n == 0:
                    PE.wait(s_se)
                    PE.wait(s_ve)
                    scan_state["marks"] = []
                i = PE.op("matmul", [pk[bkn][:, 0:TC]],
                          [b_ident[:, :], b_xpj[:, :]],
                          pk[bkn][:, 0:TC],
                          b_ident[0:33, n:n + 1].broadcast_to([33, 128]),
                          b_xpj[:, :], start=True, stop=True)
                s_pe.inc(i, 1)
                i = PE.op("matmul", [pk[bkn][:, TC:2 * TC]],
                          [b_ident[:, :], b_xpj[:, :]],
                          pk[bkn][:, TC:2 * TC],
                          b_ident[0:33, NST + n:NST + n + 1].broadcast_to([33, 128]),
                          b_xpj[:, :], start=True, stop=True)
                s_pe.inc(i, 1)
                marks = scan_state["marks"]
                ysrc, ydst = (b_yA, b_yB) if n % 2 == 0 else (b_yB, b_yA)
                dab = b_da if n % 2 == 0 else b_po
                for d in range(8):
                    k = n * 8 + d
                    if k >= 16:
                        SE.wait_v(s_ve, marks[k - 16])
                    i = SE.op("activation", [dab[:, d, :]],
                              [b_dt[:, d, :], b_A[:, d, n:n + 1]],
                              dab[:, d, :], b_dt[:, d, :], AF.Exp,
                              scale=b_A[:, d, n:n + 1])
                    s_se.inc(i, 1)
                VE.wait(s_pe)
                VE.wait(s_se)
                for dh in range(2):
                    d0 = dh * 4
                    dbx = b_dbx[:, dh, :, :]
                    VE.wait(s_gp)
                    i = VE.op("tensor_tensor", [dbx],
                              [b_s1T[:, 0, :]],
                              dbx, b_s1T[:, d0:d0 + 4, :],
                              pk[bkn][:, 0:TC].unsqueeze(1).broadcast_to([128, 4, TC]),
                              ALU.mult)
                    s_ve.inc(i, 1)
                    for dd in range(4):
                        d = d0 + dd
                        strip = n * 8 + d
                        i = VE.op("tensor_tensor_scan", [b_xc[:, d, :]],
                                  [dab[:, d, :], dbx[:, dd, :],
                                   b_state[:, strip:strip + 1]],
                                  b_xc[:, d, :], dab[:, d, :], dbx[:, dd, :],
                                  b_state[:, strip:strip + 1], ALU.mult, ALU.add)
                        s_ve.inc(i, 1)
                        marks.append(s_ve.n)
                    i = VE.op("tensor_tensor", [dbx], [b_xc[:, 0, :]],
                              dbx, b_xc[:, d0:d0 + 4, :],
                              pk[bkn][:, TC:2 * TC].unsqueeze(1).broadcast_to([128, 4, TC]),
                              ALU.mult)
                    s_ve.inc(i, 1)
                    GP.wait(s_ve)
                    i = GP.op("tensor_tensor", [ydst[:, d0, :]],
                              [ysrc[:, 0, :], dbx[:, 0, :]],
                              ydst[:, d0:d0 + 4, :], ysrc[:, d0:d0 + 4, :],
                              dbx, ALU.add)
                    s_gp.inc(i, 1)
                bank_release(bkn, s_ve)
                i = VE.op("tensor_copy", [b_state[:, n * 8:(n + 1) * 8]],
                          [b_xc[:, :, TC - 1:TC]],
                          b_state[:, n * 8:(n + 1) * 8], b_xc[:, :, TC - 1:TC])
                s_ve.inc(i, 1)
                if n == NST - 1:
                    scan_ue_mark[0] = s_ve.n

            def seg_wout(ch):
                VE.wait(s_gp)
                for kc in range(8):
                    i = VE.op("tensor_mul", [b_po[:, kc, :]],
                              [b_yA[:, kc, :], b_zs[:, kc, :]],
                              b_po[:, kc, :], b_yA[:, kc, :], b_zs[:, kc, :])
                    s_ve.inc(i, 1)
                g2 = ch * 8
                SY.wait_v(s_wof, max(0, g2 - 1))
                sload(st_wo, g2, b_wos[:, g2 % 2, :], WOUT[:, 0, :])
                for kc in range(8):
                    g2 = ch * 8 + kc
                    swait(PE, st_wo, g2)
                    if kc == 0:
                        PE.wait(s_ve)
                        for bi in range(8):
                            bank_wait(bi)
                    for m in range(8):
                        i = PE.op("matmul", [pk[m][:, 0:TC]],
                                  [b_wos[:, g2 % 2, m * 128:(m + 1) * 128],
                                   b_po[:, kc, :]],
                                  pk[m][:, 0:TC],
                                  b_wos[:, g2 % 2, m * 128:(m + 1) * 128],
                                  b_po[:, kc, :], start=(kc == 0), stop=(kc == 7))
                        if m == 3 and g2 >= 1:
                            s_wof.inc(i, 1)
                        else:
                            s_pe.inc(i, 1)
                    if kc + 1 < 8:
                        ng = g2 + 1
                        SY.wait_v(s_wof, g2)
                        sload(st_wo, ng, b_wos[:, ng % 2, :], WOUT[:, kc + 1, :])
                SE.wait(s_pe)
                for m in range(8):
                    i = SE.op("copy", [b_po[:, m, :]], [pk[m][:, 0:TC]],
                              b_po[:, m, :], pk[m][:, 0:TC])
                    s_se.inc(i, 1)
                    bank_release(m, s_se)
                for tt in range(NTT):
                    for d in range(NMT):
                        bki = 6 + (d % 2)
                        bk = pk[bki]
                        if tt == 0 and d == 0:
                            PE.wait(s_se)
                        bank_wait(bki)
                        i = PE.op("transpose", [bk[:, 0:128]],
                                  [b_po[:, d, :], b_ident[:, :]],
                                  bk[:, 0:128],
                                  b_po[:, d, tt * 128:(tt + 1) * 128],
                                  b_ident[:, :])
                        s_pe.inc(i, 1)
                        SE.wait(s_pe)
                        i = SE.op("copy", [b_yA[:, 0, :]], [bk[:, 0:128]],
                                  bass.AP(b_yA, tt * DM + d * 128,
                                          [[8 * TC, 128], [1, 128]]),
                                  bk[:, 0:128])
                        s_se.inc(i, 1)
                        bank_release(bki, s_se)
                SY.wait(s_se)
                dma(ccmi[:, :], b_yA[:, :, :], s_cci)
                GP.wait(s_cci)
                i = None
                if GP.e is not None:
                    i = GP.e.collective_compute(
                        "AllReduce", ALU.add, replica_groups=groups,
                        ins=[ccmi[:, :]], outs=[ccmo[:, :]])
                s_cc.inc(i, 1)
                SY.wait(s_cc)
                dma(b_yA[:, :, :], ccmo[:, :], s_cco)

            def seg_sandwich(ch):
                for tt in range(NTT):
                    VE.wait(s_cco)
                    i = VE.op("tensor_tensor", [b_tma[:, tt, :]],
                              [b_xn[:, tt, :], b_yA[:, 0, :]],
                              b_tma[:, tt, :], b_xn[:, tt, :],
                              bass.AP(b_yA, tt * DM, [[8 * TC, 128], [1, DM]]),
                              ALU.subtract)
                    s_ve.inc(i, 1)
                    layernorm_tm(b_tma[:, tt, :], b_xn[:, tt, :])
                    i = VE.op("tensor_add", [b_x1[:, tt, :]],
                              [b_xn[:, tt, :], b_x0[:, tt, :]],
                              b_x1[:, tt, :], b_xn[:, tt, :], b_x0[:, tt, :])
                    s_ve.inc(i, 1)
                    layernorm_tm(b_x1[:, tt, :], b_x0[:, tt, :])
                    for d in range(NMT):
                        bki = 6 + (d % 2)
                        bk = pk[bki]
                        PE.wait(s_ve)
                        bank_wait(bki)
                        i = PE.op("transpose", [bk[:, 0:128]],
                                  [b_x0[:, tt, :], b_ident[:, :]],
                                  bk[:, 0:128],
                                  b_x0[:, tt, d * 128:(d + 1) * 128],
                                  b_ident[:, :])
                        s_pe.inc(i, 1)
                        SE.wait(s_pe)
                        i = SE.op("copy",
                                  [b_s1T[:, d, tt * 128:(tt + 1) * 128]],
                                  [bk[:, 0:128]],
                                  b_s1T[:, d, tt * 128:(tt + 1) * 128],
                                  bk[:, 0:128])
                        s_se.inc(i, 1)
                        bank_release(bki, s_se)
                mark_fin[0] = s_se.n
                PE.free(s_xf)

            def seg_wm_router(ch):
                g2 = ch * 8
                SY.wait_v(s_wmf, max(0, g2 - 1))
                sload(st_wm, g2, b_wms[:, g2 % 2, :], WM[:, 0, :])
                for kc in range(8):
                    g2 = ch * 8 + kc
                    swait(PE, st_wm, g2)
                    if kc == 0:
                        PE.wait_v(s_se, mark_fin[0])
                        for bi in range(NH):
                            bank_wait(bi)
                    for j in range(NH):
                        i = PE.op("matmul", [pk[j][:, 0:TC]],
                                  [b_wms[:, g2 % 2, j * 128:(j + 1) * 128],
                                   b_s1T[:, kc, :]],
                                  pk[j][:, 0:TC],
                                  b_wms[:, g2 % 2, j * 128:(j + 1) * 128],
                                  b_s1T[:, kc, :], start=(kc == 0), stop=(kc == 7))
                        if j == 2 and g2 >= 1:
                            s_wmf.inc(i, 1)
                        else:
                            s_pe.inc(i, 1)
                    if kc + 1 < 8:
                        ng = g2 + 1
                        SY.wait_v(s_wmf, g2)
                        sload(st_wm, ng, b_wms[:, ng % 2, :], WM[:, kc + 1, :])
                SE.wait(s_pe)
                for j in range(NH):
                    i = SE.op("activation", [b_hf[:, j, :]],
                              [pk[j][:, 0:TC], b_bm[:, j:j + 1]],
                              b_hf[:, j, :], pk[j][:, 0:TC], AF.Identity,
                              bias=b_bm[:, j:j + 1])
                    s_se.inc(i, 1)
                    bank_release(j, s_se)
                for j in range(NH):
                    PE.wait(s_se)
                    bank_wait(2)
                    i = PE.op("matmul", [pk[2][0:NST, 0:TC]],
                              [b_emb[:, :], b_hf[:, j, :]],
                              pk[2][0:NST, 0:TC], b_emb[:, :], b_hf[:, j, :],
                              start=True, stop=True)
                    s_pe.inc(i, 1)
                    SE.wait(s_pe)
                    i = SE.op("copy", [b_lg[:, :]], [pk[2][0:NST, 0:TC]],
                              b_lg[:, :], pk[2][0:NST, 0:TC])
                    s_se.inc(i, 1)
                    bank_release(2, s_se)
                    for tt2 in range(NTT):
                        PE.wait(s_se)
                        bank_wait(3)
                        i = PE.op("transpose", [pk[3][0:128, 0:NST]],
                                  [b_lg[:, :], b_ident[:, :]],
                                  pk[3][0:128, 0:NST],
                                  b_lg[:, tt2 * 128:(tt2 + 1) * 128],
                                  b_ident[0:NST, 0:NST])
                        s_pe.inc(i, 1)
                        VE.wait(s_pe)
                        i = VE.op("tensor_copy", [b_lt[:, :]], [pk[3][0:128, 0:NST]],
                                  b_lt[:, :], pk[3][0:128, 0:NST])
                        s_ve.inc(i, 1)
                        bank_release(3, s_ve)
                        i = VE.op("reduce_max", [b_m1[:, 0:1]], [b_lt[:, :]],
                                  b_m1[:, 0:1], b_lt[:, :], mybir.AxisListType.X)
                        s_ve.inc(i, 1)
                        i = VE.op("tensor_scalar_mul", [b_m1[:, 1:2]],
                                  [b_m1[:, 0:1], neg1],
                                  b_m1[:, 1:2], b_m1[:, 0:1], neg1)
                        s_ve.inc(i, 1)
                        SE.wait(s_ve)
                        i = SE.op("activation", [b_w16[:, :], b_m1[:, 3:4]],
                                  [b_lt[:, :], b_m1[:, 1:2]],
                                  b_w16[:, :], b_lt[:, :], AF.Exp,
                                  bias=b_m1[:, 1:2], accum_out=b_m1[:, 3:4])
                        s_se.inc(i, 1)
                        VE.wait(s_se)
                        i = VE.op("reciprocal", [b_rec[:, :]], [b_m1[:, 3:4]],
                                  b_rec[:, :], b_m1[:, 3:4])
                        s_ve.inc(i, 1)
                        i = VE.op("tensor_single_scalar", [b_i2[:, :]],
                                  [b_lt[:, :], b_m1[:, 0:1]],
                                  b_i2[:, :], b_lt[:, :], b_m1[:, 0:1], ALU.is_ge)
                        s_ve.inc(i, 1)
                        i = VE.op("tensor_scalar_mul", [b_t1[:, :]],
                                  [b_i2[:, :], bigc],
                                  b_t1[:, :], b_i2[:, :], bigc)
                        s_ve.inc(i, 1)
                        i = VE.op("tensor_sub", [b_ltm[:, :]], [b_lt[:, :], b_t1[:, :]],
                                  b_ltm[:, :], b_lt[:, :], b_t1[:, :])
                        s_ve.inc(i, 1)
                        i = VE.op("reduce_max", [b_m1[:, 2:3]], [b_ltm[:, :]],
                                  b_m1[:, 2:3], b_ltm[:, :], mybir.AxisListType.X)
                        s_ve.inc(i, 1)
                        i = VE.op("tensor_single_scalar", [b_i2[:, :]],
                                  [b_lt[:, :], b_m1[:, 2:3]],
                                  b_i2[:, :], b_lt[:, :], b_m1[:, 2:3], ALU.is_ge)
                        s_ve.inc(i, 1)
                        i = VE.op("scalar_tensor_tensor", [b_w16[:, :]],
                                  [b_w16[:, :], b_rec[:, 0:1], b_i2[:, :]],
                                  b_w16[:, :], b_w16[:, :], b_rec[:, 0:1],
                                  b_i2[:, :], ALU.mult, ALU.mult)
                        s_ve.inc(i, 1)
                        PE.wait(s_ve)
                        bank_wait(3)
                        i = PE.op("transpose", [pk[3][0:NST, 0:128]],
                                  [b_w16[:, :], b_ident[:, :]],
                                  pk[3][0:NST, 0:128], b_w16[:, :],
                                  b_ident[:, :])
                        s_pe.inc(i, 1)
                        SE.wait(s_pe)
                        i = SE.op("copy",
                                  [b_mem[:, j * TC + tt2 * 128: j * TC + (tt2 + 1) * 128]],
                                  [pk[3][0:NST, 0:128]],
                                  b_mem[:, j * TC + tt2 * 128: j * TC + (tt2 + 1) * 128],
                                  pk[3][0:NST, 0:128])
                        s_se.inc(i, 1)
                        bank_release(3, s_se)
                for pre in range(2):
                    ge = ch * E + pre
                    SY.wait_v(s_w1f, max(0, ge - 2))
                    sload(st_w1, ge, b_w1s[:, ge % 3, :], W1[:, pre, :])
                    SY.wait_v(s_w2f, max(0, ge - 2))
                    sload(st_w2, ge, b_w2s[:, ge % 3, :, :], W2[:, pre, :, :])

            def hmid_phase(ch, idx, use4=False):
                e, j = divmod(idx, NH)
                ge = ch * E + e
                slot = ge % 3
                par = idx % 2
                if j == 0:
                    swait(PE, st_w1, ge)
                    swait(PE, st_w2, ge)
                rot = (0, 1, 2, 6) if use4 else (0, 1, 2)
                for it in range(4):
                    bki = rot[(hmq[0]) % len(rot)]
                    hmq[0] += 1
                    bank_wait(bki)
                    i = PE.op("matmul", [pk[bki][:, 0:TC]],
                              [b_w1s[:, slot, it * 128:(it + 1) * 128],
                               b_hf[:, j, :]],
                              pk[bki][:, 0:TC],
                              b_w1s[:, slot, it * 128:(it + 1) * 128],
                              b_hf[:, j, :], start=True, stop=True)
                    s_pe.inc(i, 1)
                    SE.wait(s_pe)
                    i = SE.op("activation", [b_hm[:, par, it, :]],
                              [pk[bki][:, 0:TC], b_b1[:, e, it:it + 1]],
                              b_hm[:, par, it, :], pk[bki][:, 0:TC], AF.Gelu,
                              bias=b_b1[:, e, it:it + 1])
                    s_se.inc(i, 1)
                    bank_release(bki, s_se)
                hm4_mark[idx] = s_se.n
                if j == NH - 1:
                    ne = e + 2
                    if ne < E:
                        ng = ch * E + ne
                        SY.wait_v(s_w1f, ge)
                        sload(st_w1, ng, b_w1s[:, ng % 3, :], W1[:, ne, :])
                        SY.wait_v(s_w2f, ge)
                        sload(st_w2, ng, b_w2s[:, ng % 3, :, :], W2[:, ne, :, :])

            def eo_phase(ch, idx):
                e2, j2 = divmod(idx, NH)
                ge2 = ch * E + e2
                slot2 = ge2 % 3
                par = idx % 2
                par4 = idx % 4
                PE.wait_v(s_se, hm4_mark[idx])
                bank_wait(7)
                for it in range(4):
                    i = PE.op("matmul", [pk[7][:, 0:TC]],
                              [b_w2s[:, slot2, it, :], b_hm[:, par, it, :]],
                              pk[7][:, 0:TC], b_w2s[:, slot2, it, :],
                              b_hm[:, par, it, :],
                              start=(it == 0), stop=(it == 3))
                    if j2 == 0 and it == 1 and ge2 >= 1:
                        s_w1f.inc(i, 1)
                    elif j2 == 0 and it == 2 and ge2 >= 1:
                        s_w2f.inc(i, 1)
                    else:
                        s_pe.inc(i, 1)
                SE.wait(s_pe)
                SE.wait_v(s_ve, eob_mark[par4])
                eob = b_eob[:, par4, :]
                i = SE.op("activation", [eob],
                          [pk[7][:, 0:TC], b_b2[:, e2:e2 + 1]],
                          eob, pk[7][:, 0:TC], AF.Identity,
                          bias=b_b2[:, e2:e2 + 1])
                s_se.inc(i, 1)
                bank_release(7, s_se)
                bank_wait(3)
                i = PE.op("matmul", [pk[3][:, 0:TC]],
                          [b_ident[:, :], b_mem[:, :]],
                          pk[3][:, 0:TC],
                          b_ident[0:NST, e2:e2 + 1].broadcast_to([NST, 128]),
                          b_mem[:, j2 * TC:(j2 + 1) * TC],
                          start=True, stop=True)
                s_pe.inc(i, 1)
                SE.wait(s_pe)
                SE.wait_v(s_ve, mbc_mark[par4])
                i = SE.op("copy", [b_mbc[:, par4, :]], [pk[3][:, 0:TC]],
                          b_mbc[:, par4, :], pk[3][:, 0:TC])
                s_se.inc(i, 1)
                bank_release(3, s_se)
                VE.wait(s_se)
                if idx < NH:
                    i = VE.op("tensor_tensor", [b_cmA[:, j2, :]],
                              [eob, b_mbc[:, par4, :]],
                              b_cmA[:, j2, :], eob, b_mbc[:, par4, :], ALU.mult)
                    s_ve.inc(i, 1)
                    eob_mark[par4] = s_ve.n
                    mbc_mark[par4] = s_ve.n
                else:
                    i = VE.op("tensor_tensor", [b_gt[:, :]],
                              [eob, b_mbc[:, par4, :]],
                              b_gt[:, :], eob, b_mbc[:, par4, :], ALU.mult)
                    s_ve.inc(i, 1)
                    eob_mark[par4] = s_ve.n
                    mbc_mark[par4] = s_ve.n
                    csrc, cdst = (b_cmA, b_cmB) if e2 % 2 == 1 else (b_cmB, b_cmA)
                    i = VE.op("tensor_add", [cdst[:, j2, :]],
                              [csrc[:, j2, :], b_gt[:, :]],
                              cdst[:, j2, :], csrc[:, j2, :], b_gt[:, :])
                    s_ve.inc(i, 1)

            def seg_wo(ch):
                comb = b_cmB
                WOROT = (0, 1, 2, 3, 6, 7)
                SE.wait(s_ve)
                for m in range(8):
                    bki = WOROT[m % 6]
                    bk = pk[bki]
                    if m == 0:
                        PE.wait_v(s_ve, comb_mark[0])
                    bank_wait(bki)
                    for kc in range(NH):
                        i = PE.op("matmul", [bk[:, 0:TC]],
                                  [b_wo[:, kc, m * 128:(m + 1) * 128], comb[:, kc, :]],
                                  bk[:, 0:TC],
                                  b_wo[:, kc, m * 128:(m + 1) * 128],
                                  comb[:, kc, :], start=(kc == 0), stop=False)
                        s_pe.inc(i, 1)
                    i = PE.op("matmul", [bk[:, 0:TC]],
                              [b_bolh[:, m * 128:(m + 1) * 128], b_onesr[:, :]],
                              bk[:, 0:TC], b_bolh[0:1, m * 128:(m + 1) * 128],
                              b_onesr[:, :], start=False, stop=True)
                    s_pe.inc(i, 1)
                    SE.wait(s_pe)
                    i = SE.op("copy", [b_xc[:, m, :]], [bk[:, 0:TC]],
                              b_xc[:, m, :], bk[:, 0:TC])
                    s_se.inc(i, 1)
                    bank_release(bki, s_se)
                SE.wait(s_gp)
                for tt in range(NTT):
                    for d in range(NMT):
                        bki = 6 + (d % 2)
                        bk = pk[bki]
                        if tt == 0 and d == 0:
                            PE.wait(s_se)
                        bank_wait(bki)
                        i = PE.op("transpose", [bk[:, 0:128]],
                                  [b_xc[:, d, :], b_ident[:, :]],
                                  bk[:, 0:128],
                                  b_xc[:, d, tt * 128:(tt + 1) * 128],
                                  b_ident[:, :])
                        s_pe.inc(i, 1)
                        SE.wait(s_pe)
                        i = SE.op("copy", [b_yB[:, 0, :]], [bk[:, 0:128]],
                                  bass.AP(b_yB, tt * DM + d * 128,
                                          [[8 * TC, 128], [1, 128]]),
                                  bk[:, 0:128])
                        s_se.inc(i, 1)
                        bank_release(bki, s_se)
                mark_outtr[0] = s_pe.n
                SY.wait(s_se)
                dma(ccfi[:, :], b_yB[:, :, :], s_cci)

            def seg_wo_ar(ch):
                GP.wait(s_cci)
                i = None
                if GP.e is not None:
                    i = GP.e.collective_compute(
                        "AllReduce", ALU.add, replica_groups=groups,
                        ins=[ccfi[:, :]], outs=[ccfo[:, :]])
                s_cc.inc(i, 1)
                SY.wait(s_cc)
                dma(b_yB[:, :, :], ccfo[:, :], s_cco)

            def seg_out(ch):
                t0 = ch * TC
                x2_flat = bass.AP(b_dt, 0, [[8 * TC, 128], [1, 4 * TC]])
                l5_flat = bass.AP(b_dt, 4 * TC, [[8 * TC, 128], [1, 4 * TC]])
                for tt in range(NTT):
                    ftm = bass.AP(b_yB, tt * DM, [[8 * TC, 128], [1, DM]])
                    VE.wait(s_cco)
                    layernorm_tm(ftm, l5_flat)
                    VE.wait(s_out)
                    i = VE.op("tensor_add", [b_dt[:, 0, :]],
                              [b_dt[:, 4, :], b_x1[:, tt, :]],
                              x2_flat, l5_flat, b_x1[:, tt, :])
                    s_ve.inc(i, 1)
                    SY.wait(s_ve)
                    i = None
                    if SY.e is not None:
                        i = SY.e.dma_start(
                            OUT[t0 + tt * 128: t0 + (tt + 1) * 128, :],
                            x2_flat)
                    s_out.inc(i, 16)

            scan_state = {"marks": []}
            mark_s1t = [0]
            mark_win = [0]
            mark_fin = [0]
            mark_outtr = [0]
            comb_mark = [0]
            PRE0 = True
            for i_it in range(NCH + 2):
                och = i_it - 2
                fch = i_it - 1
                mch = i_it
                if och >= 0:
                    seg_out(och)
                if 0 <= fch < NCH:
                    seg_sandwich(fch)
                    seg_wm_router(fch)
                if mch < NCH:
                    seg_loads_ln(mch)
                if 0 <= fch < NCH:
                    eix = [0]

                    def expert_step(k):
                        if eix[0] < E * NH:
                            hmid_phase(fch, eix[0], use4=(k >= 28))
                            if eix[0] >= 1:
                                eo_phase(fch, eix[0] - 1)
                            eix[0] += 1
                        elif eix[0] == E * NH:
                            eo_phase(fch, E * NH - 1)
                            comb_mark[0] = s_ve.n
                            eix[0] += 1

                    for idx in range(E * NH):
                        expert_step(idx)
                        if idx in (27, 29, 31, 33, 35, 37, 39, 41, 43, 45):
                            expert_step(idx)
                        if mch < NCH:
                            if idx < 16:
                                seg_win_mm(mch, idx)
                            if 1 <= idx < 17:
                                seg_win_post(mch, idx - 1)
                            elif 18 <= idx < 26:
                                seg_dt(mch, idx - 18)
                            elif 26 <= idx < 58 and (idx - 26) % 2 == 0:
                                seg_scan_n(mch, (idx - 26) // 2)
                    if eix[0] == E * NH:
                        eo_phase(fch, E * NH - 1)
                        comb_mark[0] = s_ve.n
                elif mch < NCH:
                    for j in range(16):
                        seg_win_mm(mch, j)
                        seg_win_post(mch, j)
                    for d in range(8):
                        seg_dt(mch, d)
                    for n in range(NST):
                        seg_scan_n(mch, n)
                if 0 <= fch < NCH:
                    seg_wo(fch)
                if mch < NCH:
                    seg_wout(mch)
                if 0 <= fch < NCH:
                    seg_wo_ar(fch)
                PRE0 = False

            if SY.e is not None:
                SY.e.wait_ge(s_out.h, s_out.n)

            for s in (s_ld, s_wf, s_w1f, s_w2f, s_cci, s_cc, s_cco, s_pe,
                      s_ve, s_se, s_out, s_xf, s_wof, s_wmf, s_gp,
                      *st_x, *st_win, *st_wo, *st_wm, *st_w1, *st_w2):
                s.n = 0

        @block.sync
        def _(sync):
            gen("sync", sync, None, None, None, None)

        @block.tensor
        def _(tensor):
            gen("tensor", None, tensor, None, None, None)

        @block.vector
        def _(vector):
            gen("vector", None, None, vector, None, None)

        @block.scalar
        def _(scalar):
            gen("scalar", None, None, None, scalar, None)

        @block.gpsimd
        def _(gpsimd):
            gen("gpsimd", None, None, None, None, gpsimd)

    return nc


def prep_in_maps(inputs, n_cores=8, T=4096):
    g = {k: np.asarray(v, dtype=np.float32) for k, v in inputs.items()}
    x = g["L0_attn_x"]
    W_in = g["L0_attn_W_in"]
    conv_w = g["L0_attn_conv_w"]
    conv_b = g["L0_attn_conv_b"]
    W_xp = g["L0_attn_W_xp"]
    W_dt = g["L0_attn_W_dt"]
    b_dt = g["L0_attn_b_dt"]
    A_log = g["L0_attn_A_log"]
    Dv = g["L0_attn_D"]
    W_out = g["L0_attn_W_out"]
    ln_g = g["L0_attn_ln_g"]
    ln_b = g["L0_attn_ln_b"]
    Wm = g["L0_ffn_Wm"]
    bm = g["L0_ffn_bm"]
    emb = g["L0_ffn_expert_emb"]
    W1 = g["L0_ffn_W1"]
    b1 = g["L0_ffn_b1"]
    W2 = g["L0_ffn_W2"]
    b2 = g["L0_ffn_b2"]
    Wo = g["L0_ffn_Wo"]
    bo = g["L0_ffn_bo"]

    consts = np.zeros((128, 8), np.float32)
    consts[:, C_ONE] = 1.0
    consts[:, C_NEG1] = -1.0
    consts[:, C_EPS] = 1e-5
    consts[:, C_BIG] = 1e30

    def tile_cols(v, nt):
        return np.ascontiguousarray(v.reshape(nt, 128).T)

    in_maps = []
    for c in range(n_cores):
        b = c // 2
        q = c % 2
        ch0 = q * 1024
        Wc = np.concatenate([W_in[:, ch0:ch0 + 1024],
                             W_in[:, 2048 + ch0:2048 + ch0 + 1024]], axis=1)
        win = Wc.reshape(8, 128, 16, 128)
        win = np.ascontiguousarray(win.transpose(2, 1, 0, 3))
        wxp = np.ascontiguousarray(
            W_xp[ch0:ch0 + 1024].reshape(8, 128, 33).transpose(1, 0, 2))
        alog = np.ascontiguousarray(
            A_log[ch0:ch0 + 1024].reshape(8, 128, 16).transpose(1, 0, 2))
        convw = np.ascontiguousarray(
            conv_w[ch0:ch0 + 1024].reshape(8, 128, 4).transpose(1, 0, 2))
        wout = np.ascontiguousarray(
            W_out[ch0:ch0 + 1024].reshape(8, 128, 1024).transpose(1, 0, 2))
        h0 = q * 512
        wm = np.ascontiguousarray(
            Wm[:, h0:h0 + 512].reshape(8, 128, 512).transpose(1, 0, 2))
        wo = np.ascontiguousarray(
            Wo[h0:h0 + 512].reshape(4, 128, 1024).transpose(1, 0, 2))
        w1 = np.ascontiguousarray(W1.transpose(1, 0, 2))
        w2 = np.ascontiguousarray(
            W2.reshape(16, 4, 128, 128).transpose(2, 0, 1, 3))
        b1t = np.ascontiguousarray(b1.reshape(16, 4, 128).transpose(2, 0, 1))
        b2t = np.ascontiguousarray(b2.T)
        bolh = (bo if q == 0 else np.zeros_like(bo)).reshape(1, 1024)
        m = {
            "X": np.ascontiguousarray(x[b, :T]),
            "WIN": win,
            "LNG": tile_cols(ln_g, 8),
            "LNB": tile_cols(ln_b, 8),
            "CONVW": convw,
            "CONVB": tile_cols(conv_b[ch0:ch0 + 1024], 8),
            "WXP": wxp,
            "WDT": tile_cols(-W_dt[0, ch0:ch0 + 1024], 8),
            "BDT": tile_cols(-b_dt[ch0:ch0 + 1024], 8),
            "ALOG": alog,
            "DDIN": tile_cols(-Dv[ch0:ch0 + 1024], 8),
            "WOUT": wout,
            "WM": wm,
            "BM": tile_cols(bm[h0:h0 + 512], 4),
            "EMB": emb,
            "W1": w1,
            "B1": b1t,
            "W2": w2,
            "B2": b2t,
            "WO": wo,
            "BOLH": np.ascontiguousarray(bolh),
            "ONES1": np.ones((1, 128), np.float32),
            "ONESR": np.ones((1, TC), np.float32),
            "IDENT": np.eye(128, dtype=np.float32),
            "CONSTS": consts,
        }
        in_maps.append(m)
    return in_maps


def kernel(**inputs):
    n_cores = 8
    in_maps = prep_in_maps(inputs, n_cores=n_cores)
    nc = build_nc(T=4096, n_cores=n_cores)
    res = run_bass_kernel_spmd(nc, in_maps, core_ids=list(range(n_cores)))
    out = np.stack([np.asarray(res.results[2 * b]["OUT"]) for b in range(4)])
    return out


# version 27 (final)
# speedup vs baseline: 1.0719x; 1.0673x over previous
import numpy as np
import concourse.bass as bass
import concourse.mybir as mybir
from concourse.bass_utils import run_bass_kernel_spmd

F32 = mybir.dt.float32
AF = mybir.ActivationFunctionType
ALU = mybir.AluOpType

DM = 1024
DI_H = 1024
NST = 16
TC = 256
NTT = TC // 128
NDT = DI_H // 128
NMT = DM // 128
E = 16
HDD = 128
IFF = 512
NH = 4

C_ONE, C_NEG1, C_EPS, C_BIG, C_ZERO = 0, 1, 2, 3, 4


class Sem:
    def __init__(self, h):
        self.h = h
        self.n = 0

    def inc(self, ins, k):
        if ins is not None:
            ins.then_inc(self.h, k)
        self.n += k


class Eng:
    def __init__(self, eng, active, track=True, self_sem=None):
        self.e = eng if active else None
        self.dirty = set()
        self.reads = set()
        self.track = track
        self.self_sem = self_sem

    def wait(self, sem):
        if self.e is not None and sem.n > 0:
            self.e.wait_ge(sem.h, sem.n)

    def wait_v(self, sem, v):
        if self.e is not None and v > 0:
            self.e.wait_ge(sem.h, v)

    def free(self, sem):
        i = None
        if self.e is not None:
            self.e.drain()
            i = self.e.nop()
        sem.inc(i, 1)

    def op(self, method, outs, ins, *args, **kw):
        if self.e is None:
            return None
        def nm(lst):
            out = set()
            for a in lst:
                try:
                    out.add(a.name)
                except Exception:
                    pass
            return out
        if self.track:
            no, ni = nm(outs), nm(ins)
            if (no & (self.dirty | self.reads)) or (ni & self.dirty):
                if self.self_sem is not None:
                    self.e.wait_ge(self.self_sem.h, self.self_sem.n)
                else:
                    self.e.drain()
                self.dirty.clear()
                self.reads.clear()
            ins_obj = getattr(self.e, method)(*args, **kw)
            self.dirty |= no
            self.reads |= ni
            return ins_obj
        return getattr(self.e, method)(*args, **kw)


def build_nc(T=4096, n_cores=8):
    NCH = T // TC
    groups = [[2 * i, 2 * i + 1] for i in range(n_cores // 2)]
    nc = bass.Bass("TRN2", target_bir_lowering=False, debug=False)

    X = nc.dram_tensor("X", [T, DM], F32, kind="ExternalInput")
    WIN = nc.dram_tensor("WIN", [16, 128, 8, 128], F32, kind="ExternalInput")
    LNG = nc.dram_tensor("LNG", [128, 8], F32, kind="ExternalInput")
    LNB = nc.dram_tensor("LNB", [128, 8], F32, kind="ExternalInput")
    CONVW = nc.dram_tensor("CONVW", [128, 8, 4], F32, kind="ExternalInput")
    CONVB = nc.dram_tensor("CONVB", [128, 8], F32, kind="ExternalInput")
    WXP = nc.dram_tensor("WXP", [128, 8, 33], F32, kind="ExternalInput")
    WDT = nc.dram_tensor("WDT", [128, 8], F32, kind="ExternalInput")
    BDT = nc.dram_tensor("BDT", [128, 8], F32, kind="ExternalInput")
    ALOG = nc.dram_tensor("ALOG", [128, 8, NST], F32, kind="ExternalInput")
    DDIN = nc.dram_tensor("DDIN", [128, 8], F32, kind="ExternalInput")
    WOUT = nc.dram_tensor("WOUT", [128, 8, DM], F32, kind="ExternalInput")
    WM = nc.dram_tensor("WM", [128, 8, NH * 128], F32, kind="ExternalInput")
    BM = nc.dram_tensor("BM", [128, NH], F32, kind="ExternalInput")
    EMB = nc.dram_tensor("EMB", [128, NST], F32, kind="ExternalInput")
    W1 = nc.dram_tensor("W1", [128, E, IFF], F32, kind="ExternalInput")
    B1 = nc.dram_tensor("B1", [128, E, 4], F32, kind="ExternalInput")
    W2 = nc.dram_tensor("W2", [128, E, 4, 128], F32, kind="ExternalInput")
    B2 = nc.dram_tensor("B2", [128, E], F32, kind="ExternalInput")
    WO = nc.dram_tensor("WO", [128, NH, DM], F32, kind="ExternalInput")
    BOLH = nc.dram_tensor("BOLH", [1, 8 * 128], F32, kind="ExternalInput")
    ONES1 = nc.dram_tensor("ONES1", [1, 128], F32, kind="ExternalInput")
    ONESR = nc.dram_tensor("ONESR", [1, TC], F32, kind="ExternalInput")
    IDENT = nc.dram_tensor("IDENT", [128, 128], F32, kind="ExternalInput")
    CONSTS = nc.dram_tensor("CONSTS", [128, 8], F32, kind="ExternalInput")
    OUT = nc.dram_tensor("OUT", [T, DM], F32, kind="ExternalOutput")
    ccxi = nc.dram_tensor("ccxi", [33, TC], F32)
    ccxo = nc.dram_tensor("ccxo", [33, TC], F32)
    ccmi = nc.dram_tensor("ccmi", [128, 8 * TC], F32)
    ccmo = nc.dram_tensor("ccmo", [128, 8 * TC], F32)
    ccfi = nc.dram_tensor("ccfi", [128, 8 * TC], F32)
    ccfo = nc.dram_tensor("ccfo", [128, 8 * TC], F32)

    from contextlib import ExitStack
    es = ExitStack()
    SB = lambda n_, s: es.enter_context(nc.sbuf_tensor(n_, s, F32))
    PS = lambda n_: es.enter_context(nc.psum_tensor(n_, [128, 512], F32))
    SEM = lambda n_: Sem(es.enter_context(nc.semaphore(n_)))

    with es:
        b_wos = SB("b_wos", [128, 2, DM])
        b_wms = SB("b_wms", [128, 3, NH * 128])
        b_wo = SB("b_wo", [128, NH, DM])
        b_wxp = SB("b_wxp", [128, 8, 33])
        b_A = SB("b_A", [128, 8, NST])
        b_convw = SB("b_convw", [128, 8, 4])
        b_convb = SB("b_convb", [128, 8])
        b_wdt = SB("b_wdt", [128, 8])
        b_bdt = SB("b_bdt", [128, 8])
        b_dd = SB("b_dd", [128, 8])
        b_lng = SB("b_lng", [128, 8])
        b_lnb = SB("b_lnb", [128, 8])
        b_bm = SB("b_bm", [128, NH])
        b_emb = SB("b_emb", [128, NST])
        b_b1 = SB("b_b1", [128, E, 4])
        b_b2 = SB("b_b2", [128, E])
        b_bolh = SB("b_bolh", [1, 8 * 128])
        b_ones1 = SB("b_ones1", [1, 128])
        b_onesr = SB("b_onesr", [1, TC])
        b_ident = SB("b_ident", [128, 128])
        b_c = SB("b_c", [128, 8])
        b_bvec = SB("b_bvec", [128, 16])
        b_win = SB("b_win", [128, 2, 8, 128])
        b_w1s = SB("b_w1s", [128, 3, IFF])
        b_w2s = SB("b_w2s", [128, 3, 4, 128])
        b_x0 = SB("b_x0", [128, NTT, DM])
        b_xn = SB("b_xn", [128, NTT, DM])
        b_tma = SB("b_tma", [128, NTT, DM])
        b_x1 = SB("b_x1", [128, NTT, DM])
        b_s1T = SB("b_s1T", [128, 8, TC])
        b_zs = SB("b_zs", [128, 8, TC])
        b_xc = SB("b_xc", [128, 8, TC])
        b_dt = SB("b_dt", [128, 8, TC])
        b_yA = SB("b_yA", [128, 8, TC])
        b_yB = SB("b_yB", [128, 8, TC])
        b_po = SB("b_po", [128, 8, TC])
        b_hf = SB("b_hf", [128, NH, TC])
        b_hm = SB("b_hm", [128, 2, 4, TC])
        b_cmA = SB("b_cmA", [128, NH, TC])
        b_cmB = SB("b_cmB", [128, NH, TC])
        b_eob = SB("b_eob", [128, 4, TC])
        b_tmp = SB("b_tmp", [128, TC])
        b_gt = SB("b_gt", [128, TC])
        b_mbc = SB("b_mbc", [128, 4, TC])
        b_da = SB("b_da", [128, 8, TC])
        b_dbx = SB("b_dbx", [128, 2, 4, TC])
        b_xhe = SB("b_xhe", [128, TC + 8])
        b_cv0 = SB("b_cv0", [128, TC])
        b_cv1 = SB("b_cv1", [128, TC])
        b_ubc = SB("b_ubc", [128, TC])
        b_tail = SB("b_tail", [128, 8, 3])
        b_state = SB("b_state", [128, 128])
        b_xpj = SB("b_xpj", [33, TC])
        b_lg = SB("b_lg", [16, TC])
        b_mem = SB("b_mem", [16, NH * TC])
        b_lt = SB("b_lt", [128, NST])
        b_w16 = SB("b_w16", [128, NST])
        b_i2 = SB("b_i2", [128, NST])
        b_t1 = SB("b_t1", [128, NST])
        b_ltm = SB("b_ltm", [128, NST])
        b_st = SB("b_st", [128, 12])
        b_mv = SB("b_mv", [128, 8])
        b_rs = SB("b_rs", [128, 8])
        b_m1 = SB("b_m1", [128, 4])
        b_rec = SB("b_rec", [128, 1])
        pk = [PS(f"pk{i}") for i in range(8)]

        s_ld = SEM("s_ld")
        st_x = (SEM("s_x0a"), SEM("s_x0b"))
        st_win = (SEM("s_wina"), SEM("s_winb"))
        st_wo = (SEM("s_woa"), SEM("s_wob"))
        st_wm = (SEM("s_wma"), SEM("s_wmb"))
        st_w1 = (SEM("s_w1a"), SEM("s_w1b"))
        st_w2 = (SEM("s_w2a"), SEM("s_w2b"))
        s_wf = SEM("s_wf")
        s_wof = SEM("s_wof")
        s_wmf = SEM("s_wmf")
        s_w1f = SEM("s_w1f")
        s_w2f = SEM("s_w2f")
        s_cci = SEM("s_cci")
        s_cc = SEM("s_cc")
        s_cco = SEM("s_cco")
        s_pe = SEM("s_pe")
        s_ve = SEM("s_ve")
        s_se = SEM("s_se")
        s_out = SEM("s_out")
        s_gp = SEM("s_gp")
        s_xf = SEM("s_xf")

        block = es.enter_context(nc.Block())

        def gen(active, sync_e, tensor_e, vector_e, scalar_e, gpsimd_e):
            SY = Eng(sync_e, active == "sync")
            PE = Eng(tensor_e, active == "tensor", track=False)
            VE = Eng(vector_e, active == "vector", self_sem=s_ve)
            SE = Eng(scalar_e, active == "scalar", self_sem=s_se)
            GP = Eng(gpsimd_e, active == "gpsimd")

            ones = b_c[:, C_ONE:C_ONE + 1]
            neg1 = b_c[:, C_NEG1:C_NEG1 + 1]
            epsc = b_c[:, C_EPS:C_EPS + 1]
            bigc = b_c[:, C_BIG:C_BIG + 1]

            def sload(pair, k, dst, src):
                dma(dst, src, pair[k % 2])

            def swait(E, pair, k):
                E.wait_v(pair[k % 2], 16 * (k // 2 + 1))

            def dma(dst, src, sem):
                SY.wait(sem)
                i = None
                if SY.e is not None:
                    i = SY.e.dma_start(dst, src)
                sem.inc(i, 16)

            dma(b_wo[:, :, :], WO[:, :, :], s_ld)
            dma(b_wxp[:, :, :], WXP[:, :, :], s_ld)
            dma(b_A[:, :, :], ALOG[:, :, :], s_ld)
            dma(b_convw[:, :, :], CONVW[:, :, :], s_ld)
            dma(b_convb[:, :], CONVB[:, :], s_ld)
            dma(b_wdt[:, :], WDT[:, :], s_ld)
            dma(b_bdt[:, :], BDT[:, :], s_ld)
            dma(b_dd[:, :], DDIN[:, :], s_ld)
            dma(b_lng[:, :], LNG[:, :], s_ld)
            dma(b_lnb[:, :], LNB[:, :], s_ld)
            dma(b_bm[:, :], BM[:, :], s_ld)
            dma(b_emb[:, :], EMB[:, :], s_ld)
            dma(b_b1[:, :, :], B1[:, :, :], s_ld)
            dma(b_b2[:, :], B2[:, :], s_ld)
            dma(b_bolh[:, :], BOLH[:, :], s_ld)
            dma(b_ones1[:, :], ONES1[:, :], s_ld)
            dma(b_onesr[:, :], ONESR[:, :], s_ld)
            dma(b_ident[:, :], IDENT[:, :], s_ld)
            dma(b_c[:, :], CONSTS[:, :], s_ld)
            n_pre = s_ld.n

            SE.wait(s_ld)
            VE.wait(s_ld)
            for d in range(8):
                SE.wait(s_ve)
                i = SE.op("activation", [b_cv0[:, 0:NST]], [b_A[:, d, :]],
                          b_cv0[:, 0:NST], b_A[:, d, :], AF.Exp)
                s_se.inc(i, 1)
                VE.wait(s_se)
                i = VE.op("tensor_copy", [b_A[:, d, :]], [b_cv0[:, 0:NST]],
                          b_A[:, d, :], b_cv0[:, 0:NST])
                s_ve.inc(i, 1)
            i = VE.op("memset", [b_tail[:, :, :]], [], b_tail[:, :, :], 0.0)
            s_ve.inc(i, 1)
            i = VE.op("memset", [b_state[:, :]], [], b_state[:, :], 0.0)
            s_ve.inc(i, 1)

            PE.wait(s_ld)
            GP.wait(s_ld)
            sload(st_win, 0, b_win[:, 0, :, :], WIN[0, :, :, :])
            sload(st_win, 1, b_win[:, 1, :, :], WIN[1, :, :, :])

            out_dmas = 0
            tma_mark = [0]
            eob_mark = [0, 0, 0, 0]
            mbc_mark = [0, 0, 0, 0]
            bmark = {}

            def bank_wait(i):
                m = bmark.get(i)
                if m is not None:
                    PE.wait_v(m[0], m[1])

            def bank_release(i, sem):
                bmark[i] = (sem, sem.n)

            ln_rot = [0]

            def layernorm_tm(src_tile, dst_tile, pe_mark=None):
                sl = (ln_rot[0] % 4) * 2
                ln_rot[0] += 1
                mv = b_mv[:, sl:sl + 2]
                i = VE.op("bn_stats", [b_st[:, 0:6]], [src_tile[:, 0:512]],
                          b_st[:, 0:6], src_tile[:, 0:512])
                s_ve.inc(i, 1)
                i = VE.op("bn_stats", [b_st[:, 6:12]], [src_tile[:, 512:1024]],
                          b_st[:, 6:12], src_tile[:, 512:1024])
                s_ve.inc(i, 1)
                i = VE.op("bn_aggr", [b_mv[:, 0:2]], [b_st[:, :]],
                          mv, b_st[:, :])
                s_ve.inc(i, 1)
                SE.wait(s_ve)
                i = SE.op("activation", [b_rs[:, 1:2]], [b_mv[:, 0:2], epsc],
                          b_rs[:, sl + 1:sl + 2], mv[:, 1:2], AF.Sqrt,
                          bias=epsc)
                s_se.inc(i, 1)
                VE.wait(s_se)
                i = VE.op("reciprocal", [b_rs[:, 0:1]], [b_rs[:, 1:2]],
                          b_rs[:, sl:sl + 1], b_rs[:, sl + 1:sl + 2])
                s_ve.inc(i, 1)
                if pe_mark is not None:
                    VE.wait_v(s_pe, pe_mark)
                i = VE.op("tensor_scalar", [dst_tile],
                          [src_tile, b_mv[:, 0:2], b_rs[:, 0:1]],
                          dst_tile, src_tile, mv[:, 0:1], b_rs[:, sl:sl + 1],
                          ALU.subtract, ALU.mult)
                s_ve.inc(i, 1)

            hm4_mark = {}
            hmq = [0]
            scan_ue_mark = [0]

            def seg_loads_ln(ch):
                if ch > 0:
                    SY.wait(s_xf)
                    SE.wait(s_out)
                    VE.wait(s_out)
                t0 = ch * TC
                for tt in range(NTT):
                    sload(st_x, ch * NTT + tt, b_x0[:, tt, :],
                          X[t0 + tt * 128: t0 + (tt + 1) * 128, :])
                VE.wait(s_ld)
                SE.wait(s_ld)
                for tt in range(NTT):
                    swait(VE, st_x, ch * NTT + tt)
                    swait(SE, st_x, ch * NTT + tt)
                    layernorm_tm(b_x0[:, tt, :], b_xn[:, tt, :])
                    layernorm_tm(b_xn[:, tt, :], b_tma[:, tt, :],
                                 pe_mark=tma_mark[0])
                    for d in range(NMT):
                        bki = 6 + (d % 2)
                        bk = pk[bki]
                        if d == 0:
                            PE.wait(s_ve)
                        bank_wait(bki)
                        i = PE.op("transpose", [bk[:, 0:128]],
                                  [b_tma[:, tt, :], b_ident[:, :]],
                                  bk[:, 0:128],
                                  b_tma[:, tt, d * 128:(d + 1) * 128],
                                  b_ident[:, :])
                        s_pe.inc(i, 1)
                        SE.wait(s_pe)
                        SE.wait_v(s_ve, scan_ue_mark[0])
                        i = SE.op("activation",
                                  [b_s1T[:, d, tt * 128:(tt + 1) * 128]],
                                  [bk[:, 0:128], b_lng[:, d:d + 1]],
                                  b_s1T[:, d, tt * 128:(tt + 1) * 128],
                                  bk[:, 0:128], AF.Identity,
                                  scale=b_lng[:, d:d + 1])
                        s_se.inc(i, 1)
                        bank_release(bki, s_se)
                    tma_mark[0] = s_pe.n
                mark_s1t[0] = s_se.n

            def seg_win_mm(ch, j):
                g = ch * 16 + j
                slot = g % 2
                bki = 4 + (j % 2)
                bk = pk[bki]
                swait(PE, st_win, g)
                bank_wait(bki)
                if j == 0:
                    PE.wait_v(s_se, mark_s1t[0])
                for kc in range(8):
                    i = PE.op("matmul", [bk[:, 0:TC]],
                              [b_win[:, slot, kc, :], b_s1T[:, kc, :]],
                              bk[:, 0:TC], b_win[:, slot, kc, :],
                              b_s1T[:, kc, :],
                              start=(kc == 0), stop=(kc == 7))
                    if kc == 3 and g >= 1:
                        s_wf.inc(i, 1)
                    else:
                        s_pe.inc(i, 1)
                if ch == 0 and PRE0:
                    bank_wait(7)
                    for kc in range(8):
                        i = PE.op("matmul", [pk[7][:, 0:1]],
                                  [b_win[:, slot, kc, :], b_lnb[:, kc:kc + 1]],
                                  pk[7][:, 0:1], b_win[:, slot, kc, :],
                                  b_lnb[:, kc:kc + 1],
                                  start=(kc == 0), stop=(kc == 7))
                        s_pe.inc(i, 1)
                    SE.wait(s_pe)
                    i = SE.op("copy", [b_bvec[:, j:j + 1]], [pk[7][:, 0:1]],
                              b_bvec[:, j:j + 1], pk[7][:, 0:1])
                    s_se.inc(i, 1)
                    bank_release(7, s_se)
                nj = g + 1
                if g > 0 and nj < NCH * 16:
                    SY.wait_v(s_wf, g)
                    sload(st_win, nj, b_win[:, nj % 2, :, :], WIN[nj % 16, :, :, :])

            def seg_win_post(ch, j):
                g = ch * 16 + j
                bki = 4 + (j % 2)
                bk = pk[bki]
                SE.wait(s_pe)
                if j < 8:
                    d = j
                    i = SE.op("activation", [b_xhe[:, 3:3 + TC]],
                              [bk[:, 0:TC], b_bvec[:, j:j + 1]],
                              b_xhe[:, 3:3 + TC], bk[:, 0:TC], AF.Identity,
                              bias=b_bvec[:, j:j + 1])
                    s_se.inc(i, 1)
                    bank_release(bki, s_se)
                    VE.wait(s_se)
                    i = VE.op("tensor_copy", [b_xhe[:, 0:3]], [b_tail[:, d, :]],
                              b_xhe[:, 0:3], b_tail[:, d, :])
                    s_ve.inc(i, 1)
                    i = VE.op("tensor_copy", [b_tail[:, d, :]],
                              [b_xhe[:, TC:TC + 3]],
                              b_tail[:, d, :], b_xhe[:, TC:TC + 3])
                    s_ve.inc(i, 1)
                    i = VE.op("tensor_scalar", [b_cv0[:, :]],
                              [b_xhe[:, 0:TC], b_convw[:, d, 0:1], b_convb[:, d:d + 1]],
                              b_cv0[:, :], b_xhe[:, 0:TC], b_convw[:, d, 0:1],
                              b_convb[:, d:d + 1], ALU.mult, ALU.add)
                    s_ve.inc(i, 1)
                    i = VE.op("scalar_tensor_tensor", [b_cv1[:, :]],
                              [b_xhe[:, 1:1 + TC], b_convw[:, d, 1:2], b_cv0[:, :]],
                              b_cv1[:, :], b_xhe[:, 1:1 + TC],
                              b_convw[:, d, 1:2], b_cv0[:, :], ALU.mult, ALU.add)
                    s_ve.inc(i, 1)
                    i = VE.op("scalar_tensor_tensor", [b_tmp[:, :]],
                              [b_xhe[:, 2:2 + TC], b_convw[:, d, 2:3], b_cv1[:, :]],
                              b_tmp[:, :], b_xhe[:, 2:2 + TC],
                              b_convw[:, d, 2:3], b_cv1[:, :], ALU.mult, ALU.add)
                    s_ve.inc(i, 1)
                    i = VE.op("scalar_tensor_tensor", [b_cv0[:, :]],
                              [b_xhe[:, 3:3 + TC], b_convw[:, d, 3:4], b_tmp[:, :]],
                              b_cv0[:, :], b_xhe[:, 3:3 + TC],
                              b_convw[:, d, 3:4], b_tmp[:, :], ALU.mult, ALU.add)
                    s_ve.inc(i, 1)
                    SE.wait(s_ve)
                    if d == 0:
                        SE.wait_v(s_pe, mark_outtr[0])
                    i = SE.op("activation", [b_xc[:, d, :]], [b_cv0[:, :]],
                              b_xc[:, d, :], b_cv0[:, :], AF.Silu)
                    s_se.inc(i, 1)
                    VE.wait(s_se)
                    VE.wait(s_gp)
                    i = VE.op("tensor_scalar_mul", [b_yA[:, d, :]],
                              [b_xc[:, d, :], b_dd[:, d:d + 1]],
                              b_yA[:, d, :], b_xc[:, d, :], b_dd[:, d:d + 1])
                    s_ve.inc(i, 1)
                    PE.wait(s_se)
                    if d == 0:
                        bank_wait(6)
                    i = PE.op("matmul", [pk[6][0:33, 0:TC]],
                              [b_wxp[:, d, :], b_xc[:, d, :]],
                              pk[6][0:33, 0:TC], b_wxp[:, d, :], b_xc[:, d, :],
                              start=(d == 0), stop=(d == 7))
                    s_pe.inc(i, 1)
                    if d == 7:
                        VE.wait(s_pe)
                        i = VE.op("tensor_copy", [b_xpj[:, :]],
                                  [pk[6][0:33, 0:TC]],
                                  b_xpj[:, :], pk[6][0:33, 0:TC])
                        s_ve.inc(i, 1)
                        bank_release(6, s_ve)
                        SY.wait(s_ve)
                        dma(ccxi[:, :], b_xpj[:, :], s_cci)
                        GP.wait(s_cci)
                        i = None
                        if GP.e is not None:
                            i = GP.e.collective_compute(
                                "AllReduce", ALU.add, replica_groups=groups,
                                ins=[ccxi[:, :]], outs=[ccxo[:, :]])
                        s_cc.inc(i, 1)
                        SY.wait(s_cc)
                        dma(b_xpj[:, :], ccxo[:, :], s_cco)
                else:
                    d = j - 8
                    if d == 0:
                        SE.wait(s_ve)
                    i = SE.op("activation", [b_zs[:, d, :]],
                              [bk[:, 0:TC], b_bvec[:, j:j + 1]],
                              b_zs[:, d, :], bk[:, 0:TC], AF.Silu,
                              bias=b_bvec[:, j:j + 1])
                    s_se.inc(i, 1)
                    bank_release(bki, s_se)
                if j == 15:
                    mark_win[0] = s_pe.n

            def seg_dt(ch, d):
                if d == 0:
                    SE.wait(s_out)
                    PE.wait(s_cco)
                    bank_wait(6)
                    i = PE.op("matmul", [pk[6][:, 0:TC]],
                              [b_ident[:, :], b_xpj[:, :]],
                              pk[6][:, 0:TC],
                              b_ident[0:33, 32:33].broadcast_to([33, 128]),
                              b_xpj[:, :], start=True, stop=True)
                    s_pe.inc(i, 1)
                    SE.wait(s_pe)
                    i = SE.op("copy", [b_ubc[:, :]], [pk[6][:, 0:TC]],
                              b_ubc[:, :], pk[6][:, 0:TC])
                    s_se.inc(i, 1)
                    bank_release(6, s_se)
                SE.wait(s_ve)
                i = SE.op("activation", [b_cv0[:, :]],
                          [b_ubc[:, :], b_wdt[:, d:d + 1], b_bdt[:, d:d + 1]],
                          b_cv0[:, :], b_ubc[:, :], AF.Sigmoid,
                          scale=b_wdt[:, d:d + 1], bias=b_bdt[:, d:d + 1])
                s_se.inc(i, 1)
                i = SE.op("activation", [b_dt[:, d, :]], [b_cv0[:, :]],
                          b_dt[:, d, :], b_cv0[:, :], AF.Ln)
                s_se.inc(i, 1)
                VE.wait(s_se)
                if d == 0:
                    VE.wait_v(s_pe, mark_win[0])
                i = VE.op("tensor_mul", [b_s1T[:, d, :]], [b_dt[:, d, :], b_xc[:, d, :]],
                          b_s1T[:, d, :], b_dt[:, d, :], b_xc[:, d, :])
                s_ve.inc(i, 1)

            def seg_scan_n(ch, n):
                bkn = 4 + (n % 2)
                bank_wait(bkn)
                if n == 0:
                    PE.wait(s_se)
                    PE.wait(s_ve)
                    scan_state["marks"] = []
                i = PE.op("matmul", [pk[bkn][:, 0:TC]],
                          [b_ident[:, :], b_xpj[:, :]],
                          pk[bkn][:, 0:TC],
                          b_ident[0:33, n:n + 1].broadcast_to([33, 128]),
                          b_xpj[:, :], start=True, stop=True)
                s_pe.inc(i, 1)
                i = PE.op("matmul", [pk[bkn][:, TC:2 * TC]],
                          [b_ident[:, :], b_xpj[:, :]],
                          pk[bkn][:, TC:2 * TC],
                          b_ident[0:33, NST + n:NST + n + 1].broadcast_to([33, 128]),
                          b_xpj[:, :], start=True, stop=True)
                s_pe.inc(i, 1)
                marks = scan_state["marks"]
                ysrc, ydst = (b_yA, b_yB) if n % 2 == 0 else (b_yB, b_yA)
                dab = b_da if n % 2 == 0 else b_po
                for d in range(8):
                    k = n * 8 + d
                    if k >= 16:
                        SE.wait_v(s_ve, marks[k - 16])
                    i = SE.op("activation", [dab[:, d, :]],
                              [b_dt[:, d, :], b_A[:, d, n:n + 1]],
                              dab[:, d, :], b_dt[:, d, :], AF.Exp,
                              scale=b_A[:, d, n:n + 1])
                    s_se.inc(i, 1)
                VE.wait(s_pe)
                VE.wait(s_se)
                for dh in range(2):
                    d0 = dh * 4
                    dbx = b_dbx[:, dh, :, :]
                    VE.wait(s_gp)
                    i = VE.op("tensor_tensor", [dbx],
                              [b_s1T[:, 0, :]],
                              dbx, b_s1T[:, d0:d0 + 4, :],
                              pk[bkn][:, 0:TC].unsqueeze(1).broadcast_to([128, 4, TC]),
                              ALU.mult)
                    s_ve.inc(i, 1)
                    for dd in range(4):
                        d = d0 + dd
                        strip = n * 8 + d
                        i = VE.op("tensor_tensor_scan", [b_xc[:, d, :]],
                                  [dab[:, d, :], dbx[:, dd, :],
                                   b_state[:, strip:strip + 1]],
                                  b_xc[:, d, :], dab[:, d, :], dbx[:, dd, :],
                                  b_state[:, strip:strip + 1], ALU.mult, ALU.add)
                        s_ve.inc(i, 1)
                        marks.append(s_ve.n)
                    i = VE.op("tensor_tensor", [dbx], [b_xc[:, 0, :]],
                              dbx, b_xc[:, d0:d0 + 4, :],
                              pk[bkn][:, TC:2 * TC].unsqueeze(1).broadcast_to([128, 4, TC]),
                              ALU.mult)
                    s_ve.inc(i, 1)
                    GP.wait(s_ve)
                    i = GP.op("tensor_tensor", [ydst[:, d0, :]],
                              [ysrc[:, 0, :], dbx[:, 0, :]],
                              ydst[:, d0:d0 + 4, :], ysrc[:, d0:d0 + 4, :],
                              dbx, ALU.add)
                    s_gp.inc(i, 1)
                bank_release(bkn, s_ve)
                i = VE.op("tensor_copy", [b_state[:, n * 8:(n + 1) * 8]],
                          [b_xc[:, :, TC - 1:TC]],
                          b_state[:, n * 8:(n + 1) * 8], b_xc[:, :, TC - 1:TC])
                s_ve.inc(i, 1)
                if n == NST - 1:
                    scan_ue_mark[0] = s_ve.n

            def seg_wout(ch):
                VE.wait(s_gp)
                for kc in range(8):
                    i = VE.op("tensor_mul", [b_yB[:, kc, :]],
                              [b_yA[:, kc, :], b_zs[:, kc, :]],
                              b_yB[:, kc, :], b_yA[:, kc, :], b_zs[:, kc, :])
                    s_ve.inc(i, 1)
                g2 = ch * 8
                SY.wait_v(s_wof, max(0, g2 - 1))
                sload(st_wo, g2, b_wos[:, g2 % 2, :], WOUT[:, 0, :])
                for kc in range(8):
                    g2 = ch * 8 + kc
                    swait(PE, st_wo, g2)
                    if kc == 0:
                        PE.wait(s_ve)
                        for bi in range(8):
                            bank_wait(bi)
                    for m in range(8):
                        i = PE.op("matmul", [pk[m][:, 0:TC]],
                                  [b_wos[:, g2 % 2, m * 128:(m + 1) * 128],
                                   b_yB[:, kc, :]],
                                  pk[m][:, 0:TC],
                                  b_wos[:, g2 % 2, m * 128:(m + 1) * 128],
                                  b_yB[:, kc, :], start=(kc == 0), stop=(kc == 7))
                        if m == 3 and g2 >= 1:
                            s_wof.inc(i, 1)
                        else:
                            s_pe.inc(i, 1)
                    if kc + 1 < 8:
                        ng = g2 + 1
                        SY.wait_v(s_wof, g2)
                        sload(st_wo, ng, b_wos[:, ng % 2, :], WOUT[:, kc + 1, :])
                SE.wait(s_pe)
                for m in range(8):
                    i = SE.op("copy", [b_po[:, m, :]], [pk[m][:, 0:TC]],
                              b_po[:, m, :], pk[m][:, 0:TC])
                    s_se.inc(i, 1)
                    bank_release(m, s_se)
                for tt in range(NTT):
                    for d in range(NMT):
                        bki = 6 + (d % 2)
                        bk = pk[bki]
                        if tt == 0 and d == 0:
                            PE.wait(s_se)
                        bank_wait(bki)
                        i = PE.op("transpose", [bk[:, 0:128]],
                                  [b_po[:, d, :], b_ident[:, :]],
                                  bk[:, 0:128],
                                  b_po[:, d, tt * 128:(tt + 1) * 128],
                                  b_ident[:, :])
                        s_pe.inc(i, 1)
                        SE.wait(s_pe)
                        i = SE.op("copy", [b_yB[:, 0, :]], [bk[:, 0:128]],
                                  bass.AP(b_yB, tt * DM + d * 128,
                                          [[8 * TC, 128], [1, 128]]),
                                  bk[:, 0:128])
                        s_se.inc(i, 1)
                        bank_release(bki, s_se)
                SY.wait(s_se)
                dma(ccmi[:, :], b_yB[:, :, :], s_cci)
                GP.wait(s_cci)
                i = None
                if GP.e is not None:
                    i = GP.e.collective_compute(
                        "AllReduce", ALU.add, replica_groups=groups,
                        ins=[ccmi[:, :]], outs=[ccmo[:, :]])
                s_cc.inc(i, 1)
                SY.wait(s_cc)
                dma(b_yB[:, :, :], ccmo[:, :], s_cco)

            def seg_sandwich(ch):
                for tt in range(NTT):
                    VE.wait(s_cco)
                    i = VE.op("tensor_tensor", [b_tma[:, tt, :]],
                              [b_xn[:, tt, :], b_yB[:, 0, :]],
                              b_tma[:, tt, :], b_xn[:, tt, :],
                              bass.AP(b_yB, tt * DM, [[8 * TC, 128], [1, DM]]),
                              ALU.subtract)
                    s_ve.inc(i, 1)
                    layernorm_tm(b_tma[:, tt, :], b_xn[:, tt, :])
                    i = VE.op("tensor_add", [b_x1[:, tt, :]],
                              [b_xn[:, tt, :], b_x0[:, tt, :]],
                              b_x1[:, tt, :], b_xn[:, tt, :], b_x0[:, tt, :])
                    s_ve.inc(i, 1)
                    layernorm_tm(b_x1[:, tt, :], b_x0[:, tt, :])
                    for d in range(NMT):
                        bki = 6 + (d % 2)
                        bk = pk[bki]
                        PE.wait(s_ve)
                        bank_wait(bki)
                        i = PE.op("transpose", [bk[:, 0:128]],
                                  [b_x0[:, tt, :], b_ident[:, :]],
                                  bk[:, 0:128],
                                  b_x0[:, tt, d * 128:(d + 1) * 128],
                                  b_ident[:, :])
                        s_pe.inc(i, 1)
                        SE.wait(s_pe)
                        i = SE.op("copy",
                                  [b_s1T[:, d, tt * 128:(tt + 1) * 128]],
                                  [bk[:, 0:128]],
                                  b_s1T[:, d, tt * 128:(tt + 1) * 128],
                                  bk[:, 0:128])
                        s_se.inc(i, 1)
                        bank_release(bki, s_se)
                mark_fin[0] = s_se.n
                PE.free(s_xf)

            def seg_wm_router(ch):
                g2 = ch * 8
                SY.wait_v(s_wmf, max(0, g2 - 1))
                sload(st_wm, g2, b_wms[:, g2 % 2, :], WM[:, 0, :])
                for kc in range(8):
                    g2 = ch * 8 + kc
                    swait(PE, st_wm, g2)
                    if kc == 0:
                        PE.wait_v(s_se, mark_fin[0])
                        for bi in range(NH):
                            bank_wait(bi)
                    for j in range(NH):
                        i = PE.op("matmul", [pk[j][:, 0:TC]],
                                  [b_wms[:, g2 % 2, j * 128:(j + 1) * 128],
                                   b_s1T[:, kc, :]],
                                  pk[j][:, 0:TC],
                                  b_wms[:, g2 % 2, j * 128:(j + 1) * 128],
                                  b_s1T[:, kc, :], start=(kc == 0), stop=(kc == 7))
                        if j == 2 and g2 >= 1:
                            s_wmf.inc(i, 1)
                        else:
                            s_pe.inc(i, 1)
                    if kc + 1 < 8:
                        ng = g2 + 1
                        SY.wait_v(s_wmf, g2)
                        sload(st_wm, ng, b_wms[:, ng % 2, :], WM[:, kc + 1, :])
                SE.wait(s_pe)
                for j in range(NH):
                    i = SE.op("activation", [b_hf[:, j, :]],
                              [pk[j][:, 0:TC], b_bm[:, j:j + 1]],
                              b_hf[:, j, :], pk[j][:, 0:TC], AF.Identity,
                              bias=b_bm[:, j:j + 1])
                    s_se.inc(i, 1)
                    bank_release(j, s_se)
                for j in range(NH):
                    PE.wait(s_se)
                    bank_wait(2)
                    i = PE.op("matmul", [pk[2][0:NST, 0:TC]],
                              [b_emb[:, :], b_hf[:, j, :]],
                              pk[2][0:NST, 0:TC], b_emb[:, :], b_hf[:, j, :],
                              start=True, stop=True)
                    s_pe.inc(i, 1)
                    SE.wait(s_pe)
                    i = SE.op("copy", [b_lg[:, :]], [pk[2][0:NST, 0:TC]],
                              b_lg[:, :], pk[2][0:NST, 0:TC])
                    s_se.inc(i, 1)
                    bank_release(2, s_se)
                    for tt2 in range(NTT):
                        PE.wait(s_se)
                        bank_wait(3)
                        i = PE.op("transpose", [pk[3][0:128, 0:NST]],
                                  [b_lg[:, :], b_ident[:, :]],
                                  pk[3][0:128, 0:NST],
                                  b_lg[:, tt2 * 128:(tt2 + 1) * 128],
                                  b_ident[0:NST, 0:NST])
                        s_pe.inc(i, 1)
                        VE.wait(s_pe)
                        i = VE.op("tensor_copy", [b_lt[:, :]], [pk[3][0:128, 0:NST]],
                                  b_lt[:, :], pk[3][0:128, 0:NST])
                        s_ve.inc(i, 1)
                        bank_release(3, s_ve)
                        i = VE.op("reduce_max", [b_m1[:, 0:1]], [b_lt[:, :]],
                                  b_m1[:, 0:1], b_lt[:, :], mybir.AxisListType.X)
                        s_ve.inc(i, 1)
                        i = VE.op("tensor_scalar_mul", [b_m1[:, 1:2]],
                                  [b_m1[:, 0:1], neg1],
                                  b_m1[:, 1:2], b_m1[:, 0:1], neg1)
                        s_ve.inc(i, 1)
                        SE.wait(s_ve)
                        i = SE.op("activation", [b_w16[:, :], b_m1[:, 3:4]],
                                  [b_lt[:, :], b_m1[:, 1:2]],
                                  b_w16[:, :], b_lt[:, :], AF.Exp,
                                  bias=b_m1[:, 1:2], accum_out=b_m1[:, 3:4])
                        s_se.inc(i, 1)
                        VE.wait(s_se)
                        i = VE.op("reciprocal", [b_rec[:, :]], [b_m1[:, 3:4]],
                                  b_rec[:, :], b_m1[:, 3:4])
                        s_ve.inc(i, 1)
                        i = VE.op("tensor_single_scalar", [b_i2[:, :]],
                                  [b_lt[:, :], b_m1[:, 0:1]],
                                  b_i2[:, :], b_lt[:, :], b_m1[:, 0:1], ALU.is_ge)
                        s_ve.inc(i, 1)
                        i = VE.op("tensor_scalar_mul", [b_t1[:, :]],
                                  [b_i2[:, :], bigc],
                                  b_t1[:, :], b_i2[:, :], bigc)
                        s_ve.inc(i, 1)
                        i = VE.op("tensor_sub", [b_ltm[:, :]], [b_lt[:, :], b_t1[:, :]],
                                  b_ltm[:, :], b_lt[:, :], b_t1[:, :])
                        s_ve.inc(i, 1)
                        i = VE.op("reduce_max", [b_m1[:, 2:3]], [b_ltm[:, :]],
                                  b_m1[:, 2:3], b_ltm[:, :], mybir.AxisListType.X)
                        s_ve.inc(i, 1)
                        i = VE.op("tensor_single_scalar", [b_i2[:, :]],
                                  [b_lt[:, :], b_m1[:, 2:3]],
                                  b_i2[:, :], b_lt[:, :], b_m1[:, 2:3], ALU.is_ge)
                        s_ve.inc(i, 1)
                        i = VE.op("scalar_tensor_tensor", [b_w16[:, :]],
                                  [b_w16[:, :], b_rec[:, 0:1], b_i2[:, :]],
                                  b_w16[:, :], b_w16[:, :], b_rec[:, 0:1],
                                  b_i2[:, :], ALU.mult, ALU.mult)
                        s_ve.inc(i, 1)
                        PE.wait(s_ve)
                        bank_wait(3)
                        i = PE.op("transpose", [pk[3][0:NST, 0:128]],
                                  [b_w16[:, :], b_ident[:, :]],
                                  pk[3][0:NST, 0:128], b_w16[:, :],
                                  b_ident[:, :])
                        s_pe.inc(i, 1)
                        SE.wait(s_pe)
                        i = SE.op("copy",
                                  [b_mem[:, j * TC + tt2 * 128: j * TC + (tt2 + 1) * 128]],
                                  [pk[3][0:NST, 0:128]],
                                  b_mem[:, j * TC + tt2 * 128: j * TC + (tt2 + 1) * 128],
                                  pk[3][0:NST, 0:128])
                        s_se.inc(i, 1)
                        bank_release(3, s_se)
                for pre in range(2):
                    ge = ch * E + pre
                    SY.wait_v(s_w1f, max(0, ge - 2))
                    sload(st_w1, ge, b_w1s[:, ge % 3, :], W1[:, pre, :])
                    SY.wait_v(s_w2f, max(0, ge - 2))
                    sload(st_w2, ge, b_w2s[:, ge % 3, :, :], W2[:, pre, :, :])

            def hmid_phase(ch, idx, use4=False):
                e, j = divmod(idx, NH)
                ge = ch * E + e
                slot = ge % 3
                par = idx % 2
                if j == 0:
                    swait(PE, st_w1, ge)
                    swait(PE, st_w2, ge)
                rot = (0, 1, 2, 6) if use4 else (0, 1, 2)
                for it in range(4):
                    bki = rot[(hmq[0]) % len(rot)]
                    hmq[0] += 1
                    bank_wait(bki)
                    i = PE.op("matmul", [pk[bki][:, 0:TC]],
                              [b_w1s[:, slot, it * 128:(it + 1) * 128],
                               b_hf[:, j, :]],
                              pk[bki][:, 0:TC],
                              b_w1s[:, slot, it * 128:(it + 1) * 128],
                              b_hf[:, j, :], start=True, stop=True)
                    s_pe.inc(i, 1)
                    SE.wait(s_pe)
                    i = SE.op("activation", [b_hm[:, par, it, :]],
                              [pk[bki][:, 0:TC], b_b1[:, e, it:it + 1]],
                              b_hm[:, par, it, :], pk[bki][:, 0:TC], AF.Gelu,
                              bias=b_b1[:, e, it:it + 1])
                    s_se.inc(i, 1)
                    bank_release(bki, s_se)
                hm4_mark[idx] = s_se.n
                if j == NH - 1:
                    ne = e + 2
                    if ne < E:
                        ng = ch * E + ne
                        SY.wait_v(s_w1f, ge)
                        sload(st_w1, ng, b_w1s[:, ng % 3, :], W1[:, ne, :])
                        SY.wait_v(s_w2f, ge)
                        sload(st_w2, ng, b_w2s[:, ng % 3, :, :], W2[:, ne, :, :])

            def eo_phase(ch, idx):
                e2, j2 = divmod(idx, NH)
                ge2 = ch * E + e2
                slot2 = ge2 % 3
                par = idx % 2
                par4 = idx % 4
                PE.wait_v(s_se, hm4_mark[idx])
                bank_wait(7)
                for it in range(4):
                    i = PE.op("matmul", [pk[7][:, 0:TC]],
                              [b_w2s[:, slot2, it, :], b_hm[:, par, it, :]],
                              pk[7][:, 0:TC], b_w2s[:, slot2, it, :],
                              b_hm[:, par, it, :],
                              start=(it == 0), stop=(it == 3))
                    if j2 == 0 and it == 1 and ge2 >= 1:
                        s_w1f.inc(i, 1)
                    elif j2 == 0 and it == 2 and ge2 >= 1:
                        s_w2f.inc(i, 1)
                    else:
                        s_pe.inc(i, 1)
                SE.wait(s_pe)
                SE.wait_v(s_ve, eob_mark[par4])
                eob = b_eob[:, par4, :]
                i = SE.op("activation", [eob],
                          [pk[7][:, 0:TC], b_b2[:, e2:e2 + 1]],
                          eob, pk[7][:, 0:TC], AF.Identity,
                          bias=b_b2[:, e2:e2 + 1])
                s_se.inc(i, 1)
                bank_release(7, s_se)
                bank_wait(3)
                i = PE.op("matmul", [pk[3][:, 0:TC]],
                          [b_ident[:, :], b_mem[:, :]],
                          pk[3][:, 0:TC],
                          b_ident[0:NST, e2:e2 + 1].broadcast_to([NST, 128]),
                          b_mem[:, j2 * TC:(j2 + 1) * TC],
                          start=True, stop=True)
                s_pe.inc(i, 1)
                SE.wait(s_pe)
                SE.wait_v(s_ve, mbc_mark[par4])
                i = SE.op("copy", [b_mbc[:, par4, :]], [pk[3][:, 0:TC]],
                          b_mbc[:, par4, :], pk[3][:, 0:TC])
                s_se.inc(i, 1)
                bank_release(3, s_se)
                VE.wait(s_se)
                if idx < NH:
                    i = VE.op("tensor_tensor", [b_cmA[:, j2, :]],
                              [eob, b_mbc[:, par4, :]],
                              b_cmA[:, j2, :], eob, b_mbc[:, par4, :], ALU.mult)
                    s_ve.inc(i, 1)
                    eob_mark[par4] = s_ve.n
                    mbc_mark[par4] = s_ve.n
                else:
                    i = VE.op("tensor_tensor", [b_gt[:, :]],
                              [eob, b_mbc[:, par4, :]],
                              b_gt[:, :], eob, b_mbc[:, par4, :], ALU.mult)
                    s_ve.inc(i, 1)
                    eob_mark[par4] = s_ve.n
                    mbc_mark[par4] = s_ve.n
                    csrc, cdst = (b_cmA, b_cmB) if e2 % 2 == 1 else (b_cmB, b_cmA)
                    i = VE.op("tensor_add", [cdst[:, j2, :]],
                              [csrc[:, j2, :], b_gt[:, :]],
                              cdst[:, j2, :], csrc[:, j2, :], b_gt[:, :])
                    s_ve.inc(i, 1)

            def seg_wo(ch):
                comb = b_cmB
                SE.wait(s_ve)
                for m in range(8):
                    bk = pk[m % 2]
                    if m == 0:
                        PE.wait(s_ve)
                    bank_wait(m % 2)
                    for kc in range(NH):
                        i = PE.op("matmul", [bk[:, 0:TC]],
                                  [b_wo[:, kc, m * 128:(m + 1) * 128], comb[:, kc, :]],
                                  bk[:, 0:TC],
                                  b_wo[:, kc, m * 128:(m + 1) * 128],
                                  comb[:, kc, :], start=(kc == 0), stop=False)
                        s_pe.inc(i, 1)
                    i = PE.op("matmul", [bk[:, 0:TC]],
                              [b_bolh[:, m * 128:(m + 1) * 128], b_onesr[:, :]],
                              bk[:, 0:TC], b_bolh[0:1, m * 128:(m + 1) * 128],
                              b_onesr[:, :], start=False, stop=True)
                    s_pe.inc(i, 1)
                    SE.wait(s_pe)
                    i = SE.op("copy", [b_xc[:, m, :]], [bk[:, 0:TC]],
                              b_xc[:, m, :], bk[:, 0:TC])
                    s_se.inc(i, 1)
                    bank_release(m % 2, s_se)
                for tt in range(NTT):
                    for d in range(NMT):
                        bki = 6 + (d % 2)
                        bk = pk[bki]
                        if tt == 0 and d == 0:
                            PE.wait(s_se)
                        bank_wait(bki)
                        i = PE.op("transpose", [bk[:, 0:128]],
                                  [b_xc[:, d, :], b_ident[:, :]],
                                  bk[:, 0:128],
                                  b_xc[:, d, tt * 128:(tt + 1) * 128],
                                  b_ident[:, :])
                        s_pe.inc(i, 1)
                        SE.wait(s_pe)
                        i = SE.op("copy", [b_yA[:, 0, :]], [bk[:, 0:128]],
                                  bass.AP(b_yA, tt * DM + d * 128,
                                          [[8 * TC, 128], [1, 128]]),
                                  bk[:, 0:128])
                        s_se.inc(i, 1)
                        bank_release(bki, s_se)
                mark_outtr[0] = s_pe.n
                SY.wait(s_se)
                dma(ccfi[:, :], b_yA[:, :, :], s_cci)
                GP.wait(s_cci)
                i = None
                if GP.e is not None:
                    i = GP.e.collective_compute(
                        "AllReduce", ALU.add, replica_groups=groups,
                        ins=[ccfi[:, :]], outs=[ccfo[:, :]])
                s_cc.inc(i, 1)
                SY.wait(s_cc)
                dma(b_yA[:, :, :], ccfo[:, :], s_cco)

            def seg_out(ch):
                t0 = ch * TC
                x2_flat = bass.AP(b_dt, 0, [[8 * TC, 128], [1, 4 * TC]])
                l5_flat = bass.AP(b_dt, 4 * TC, [[8 * TC, 128], [1, 4 * TC]])
                for tt in range(NTT):
                    ftm = bass.AP(b_yA, tt * DM, [[8 * TC, 128], [1, DM]])
                    VE.wait(s_cco)
                    layernorm_tm(ftm, l5_flat)
                    VE.wait(s_out)
                    i = VE.op("tensor_add", [b_dt[:, 0, :]],
                              [b_dt[:, 4, :], b_x1[:, tt, :]],
                              x2_flat, l5_flat, b_x1[:, tt, :])
                    s_ve.inc(i, 1)
                    SY.wait(s_ve)
                    i = None
                    if SY.e is not None:
                        i = SY.e.dma_start(
                            OUT[t0 + tt * 128: t0 + (tt + 1) * 128, :],
                            x2_flat)
                    s_out.inc(i, 16)

            scan_state = {"marks": []}
            mark_s1t = [0]
            mark_win = [0]
            mark_fin = [0]
            mark_outtr = [0]
            PRE0 = True
            for i_it in range(NCH + 2):
                och = i_it - 2
                fch = i_it - 1
                mch = i_it
                if och >= 0:
                    seg_out(och)
                if 0 <= fch < NCH:
                    seg_sandwich(fch)
                    seg_wm_router(fch)
                if mch < NCH:
                    seg_loads_ln(mch)
                if 0 <= fch < NCH:
                    eix = [0]

                    def expert_step(k):
                        if eix[0] < E * NH:
                            hmid_phase(fch, eix[0], use4=(k >= 10))
                            if eix[0] >= 1:
                                eo_phase(fch, eix[0] - 1)
                            eix[0] += 1

                    for idx in range(E * NH):
                        expert_step(idx)
                        if idx in (29, 33, 37, 41, 45, 49, 53):
                            expert_step(idx)
                        if mch < NCH:
                            if idx < 16:
                                seg_win_mm(mch, idx)
                            if 1 <= idx < 17:
                                seg_win_post(mch, idx - 1)
                            elif 18 <= idx < 26:
                                seg_dt(mch, idx - 18)
                            elif 26 <= idx < 58 and (idx - 26) % 2 == 0:
                                seg_scan_n(mch, (idx - 26) // 2)
                    eo_phase(fch, E * NH - 1)
                elif mch < NCH:
                    for j in range(16):
                        seg_win_mm(mch, j)
                        seg_win_post(mch, j)
                    for d in range(8):
                        seg_dt(mch, d)
                    for n in range(NST):
                        seg_scan_n(mch, n)
                if mch < NCH:
                    seg_wout(mch)
                if 0 <= fch < NCH:
                    seg_wo(fch)
                PRE0 = False

            if SY.e is not None:
                SY.e.wait_ge(s_out.h, s_out.n)

            for s in (s_ld, s_wf, s_w1f, s_w2f, s_cci, s_cc, s_cco, s_pe,
                      s_ve, s_se, s_out, s_xf, s_wof, s_wmf, s_gp,
                      *st_x, *st_win, *st_wo, *st_wm, *st_w1, *st_w2):
                s.n = 0

        @block.sync
        def _(sync):
            gen("sync", sync, None, None, None, None)

        @block.tensor
        def _(tensor):
            gen("tensor", None, tensor, None, None, None)

        @block.vector
        def _(vector):
            gen("vector", None, None, vector, None, None)

        @block.scalar
        def _(scalar):
            gen("scalar", None, None, None, scalar, None)

        @block.gpsimd
        def _(gpsimd):
            gen("gpsimd", None, None, None, None, gpsimd)

    return nc


def prep_in_maps(inputs, n_cores=8, T=4096):
    g = {k: np.asarray(v, dtype=np.float32) for k, v in inputs.items()}
    x = g["L0_attn_x"]
    W_in = g["L0_attn_W_in"]
    conv_w = g["L0_attn_conv_w"]
    conv_b = g["L0_attn_conv_b"]
    W_xp = g["L0_attn_W_xp"]
    W_dt = g["L0_attn_W_dt"]
    b_dt = g["L0_attn_b_dt"]
    A_log = g["L0_attn_A_log"]
    Dv = g["L0_attn_D"]
    W_out = g["L0_attn_W_out"]
    ln_g = g["L0_attn_ln_g"]
    ln_b = g["L0_attn_ln_b"]
    Wm = g["L0_ffn_Wm"]
    bm = g["L0_ffn_bm"]
    emb = g["L0_ffn_expert_emb"]
    W1 = g["L0_ffn_W1"]
    b1 = g["L0_ffn_b1"]
    W2 = g["L0_ffn_W2"]
    b2 = g["L0_ffn_b2"]
    Wo = g["L0_ffn_Wo"]
    bo = g["L0_ffn_bo"]

    consts = np.zeros((128, 8), np.float32)
    consts[:, C_ONE] = 1.0
    consts[:, C_NEG1] = -1.0
    consts[:, C_EPS] = 1e-5
    consts[:, C_BIG] = 1e30

    def tile_cols(v, nt):
        return np.ascontiguousarray(v.reshape(nt, 128).T)

    in_maps = []
    for c in range(n_cores):
        b = c // 2
        q = c % 2
        ch0 = q * 1024
        Wc = np.concatenate([W_in[:, ch0:ch0 + 1024],
                             W_in[:, 2048 + ch0:2048 + ch0 + 1024]], axis=1)
        win = Wc.reshape(8, 128, 16, 128)
        win = np.ascontiguousarray(win.transpose(2, 1, 0, 3))
        wxp = np.ascontiguousarray(
            W_xp[ch0:ch0 + 1024].reshape(8, 128, 33).transpose(1, 0, 2))
        alog = np.ascontiguousarray(
            A_log[ch0:ch0 + 1024].reshape(8, 128, 16).transpose(1, 0, 2))
        convw = np.ascontiguousarray(
            conv_w[ch0:ch0 + 1024].reshape(8, 128, 4).transpose(1, 0, 2))
        wout = np.ascontiguousarray(
            W_out[ch0:ch0 + 1024].reshape(8, 128, 1024).transpose(1, 0, 2))
        h0 = q * 512
        wm = np.ascontiguousarray(
            Wm[:, h0:h0 + 512].reshape(8, 128, 512).transpose(1, 0, 2))
        wo = np.ascontiguousarray(
            Wo[h0:h0 + 512].reshape(4, 128, 1024).transpose(1, 0, 2))
        w1 = np.ascontiguousarray(W1.transpose(1, 0, 2))
        w2 = np.ascontiguousarray(
            W2.reshape(16, 4, 128, 128).transpose(2, 0, 1, 3))
        b1t = np.ascontiguousarray(b1.reshape(16, 4, 128).transpose(2, 0, 1))
        b2t = np.ascontiguousarray(b2.T)
        bolh = (bo if q == 0 else np.zeros_like(bo)).reshape(1, 1024)
        m = {
            "X": np.ascontiguousarray(x[b, :T]),
            "WIN": win,
            "LNG": tile_cols(ln_g, 8),
            "LNB": tile_cols(ln_b, 8),
            "CONVW": convw,
            "CONVB": tile_cols(conv_b[ch0:ch0 + 1024], 8),
            "WXP": wxp,
            "WDT": tile_cols(-W_dt[0, ch0:ch0 + 1024], 8),
            "BDT": tile_cols(-b_dt[ch0:ch0 + 1024], 8),
            "ALOG": alog,
            "DDIN": tile_cols(-Dv[ch0:ch0 + 1024], 8),
            "WOUT": wout,
            "WM": wm,
            "BM": tile_cols(bm[h0:h0 + 512], 4),
            "EMB": emb,
            "W1": w1,
            "B1": b1t,
            "W2": w2,
            "B2": b2t,
            "WO": wo,
            "BOLH": np.ascontiguousarray(bolh),
            "ONES1": np.ones((1, 128), np.float32),
            "ONESR": np.ones((1, TC), np.float32),
            "IDENT": np.eye(128, dtype=np.float32),
            "CONSTS": consts,
        }
        in_maps.append(m)
    return in_maps


def kernel(**inputs):
    n_cores = 8
    in_maps = prep_in_maps(inputs, n_cores=n_cores)
    nc = build_nc(T=4096, n_cores=n_cores)
    res = run_bass_kernel_spmd(nc, in_maps, core_ids=list(range(n_cores)))
    out = np.stack([np.asarray(res.results[2 * b]["OUT"]) for b in range(4)])
    return out
